# version 21 (final)
# baseline (speedup 1.0000x reference)
import numpy as np
import ml_dtypes
from contextlib import ExitStack

import concourse.bass as bass
import concourse.mybir as mybir
from concourse.bass_utils import run_bass_kernel_spmd

F32 = mybir.dt.float32
BF16 = mybir.dt.bfloat16
I32 = mybir.dt.int32
AF = mybir.ActivationFunctionType
ALU = mybir.AluOpType
AX = mybir.AxisListType

B, T, M = 2, 8192, 2048
G, H, DH = 4, 4, 128
FF, E = 1408, 8
TPC = 2048
NOWN = 16
NSP = 8
NFT = 11
NMC = 16
EPS = 1e-5
EXP_BIAS = -20.0
CAP = 768


class SemC:
    __slots__ = ("h", "n")

    def __init__(self, h):
        self.h = h
        self.n = 0

    def inc(self, instr, k=1):
        instr.then_inc(self.h, k)
        self.n += k
        return self.n

    def dinc(self, instr):
        return self.inc(instr, 16)


def build_program(debug=False, phases=4, p4lvl=5):
    nc = bass.Bass("TRN2", target_bir_lowering=False)
    kind_scr = "ExternalOutput" if debug else "Internal"

    xbd = nc.dram_tensor("xb", [T, M], F32, kind="ExternalInput").ap()
    xownd = nc.dram_tensor("xown", [TPC, M], F32, kind="ExternalInput").ap()
    wqd = nc.dram_tensor("wq", [M, G * H * DH], F32, kind="ExternalInput").ap()
    wkvd = nc.dram_tensor("wkv", [M, 2 * G * DH], F32, kind="ExternalInput").ap()
    waod = nc.dram_tensor("wao", [G * H, DH, M], F32, kind="ExternalInput").ap()
    gwd = nc.dram_tensor("gw", [M, E], F32, kind="ExternalInput").ap()
    wgd = nc.dram_tensor("wgt", [E, M, FF], F32, kind="ExternalInput").ap()
    wud = nc.dram_tensor("wup", [E, M, FF], F32, kind="ExternalInput").ap()
    wdd = nc.dram_tensor("wdn", [E, FF, M], F32, kind="ExternalInput").ap()
    ckd = nc.dram_tensor("ck", [T, G * 64], F32, kind="ExternalInput").ap()
    skd = nc.dram_tensor("sk", [T, G * 64], F32, kind="ExternalInput").ap()
    cqd = nc.dram_tensor("cq", [TPC, G * H * 64], F32, kind="ExternalInput").ap()
    sqd = nc.dram_tensor("sq", [TPC, G * H * 64], F32, kind="ExternalInput").ap()
    maskd = nc.dram_tensor("masks", [NSP, 8, 128, 256], BF16, kind="ExternalInput").ap()
    idf32d = nc.dram_tensor("idf32", [128, 128], F32, kind="ExternalInput").ap()
    eoffd = nc.dram_tensor("eoffr", [128, 8], F32, kind="ExternalInput").ap()
    onesd = nc.dram_tensor("ones_in", [128, 128], BF16, kind="ExternalInput").ap()
    outd = nc.dram_tensor("out", [TPC, M], F32, kind="ExternalOutput").ap()

    xnscr = nc.dram_tensor("xnscr", [T, M], BF16, kind=kind_scr).ap()
    xnoscr = nc.dram_tensor("xnoscr", [TPC, M], BF16, kind=kind_scr).ap()
    kvscr = nc.dram_tensor("kvscr", [T, 2 * G * DH], BF16, kind=kind_scr).ap()
    qscr = nc.dram_tensor("qscr", [TPC, G * H * DH], BF16, kind=kind_scr).ap()
    xmidscr = nc.dram_tensor("xmidscr", [TPC, M], F32, kind=kind_scr).ap()
    xn2scr = nc.dram_tensor("xn2scr", [TPC, M], BF16, kind=kind_scr).ap()
    xescr = nc.dram_tensor("xescr", [E * CAP, M], BF16, kind=kind_scr).ap()
    yescr = nc.dram_tensor("yescr", [E * CAP, M], BF16, kind=kind_scr).ap()

    ctx = ExitStack()
    with ctx:
        qS = SemC(ctx.enter_context(nc.semaphore("qS")))
        qA = SemC(ctx.enter_context(nc.semaphore("qA")))
        qG = SemC(ctx.enter_context(nc.semaphore("qG")))
        eT = SemC(ctx.enter_context(nc.semaphore("eT")))
        eV = SemC(ctx.enter_context(nc.semaphore("eV")))
        eS = SemC(ctx.enter_context(nc.semaphore("eS")))
        ALLSEM = (qS, qA, qG, eT, eV, eS)

        ones_sb = ctx.enter_context(nc.sbuf_tensor("ones_sb", [128, 128], BF16))
        cbias = ctx.enter_context(nc.sbuf_tensor("cbias", [128, 1], F32))
        ceps = ctx.enter_context(nc.sbuf_tensor("ceps", [128, 1], F32))
        blk = ctx.enter_context(nc.Block())

        def barrier():
            snap = [(s, s.n) for s in ALLSEM]

            def mk(eng):
                for s, n in snap:
                    if n:
                        eng.wait_ge(s.h, n)
            blk.sync(mk)
            blk.scalar(mk)
            blk.vector(mk)
            blk.tensor(mk)
            blk.gpsimd(mk)

        @blk.sync
        def _(sync):
            qS.dinc(sync.dma_start(out=ones_sb[:, :], in_=onesd[:, :]))

        @blk.vector
        def _(vector):
            eV.inc(vector.memset(cbias[:, :], EXP_BIAS))
            eV.inc(vector.memset(ceps[:, :], EPS))

        def emit_ln_tile(src_ap, dst_scr, tau, stbufs, ev):
            xbuf, xnb, bns, bna, rstd, nbuf = stbufs
            i = ev["i"]
            ev["i"] += 1
            p = i % 2
            r0 = tau * 128
            prev_apply = ev["apply"][i - 2] if i >= 2 else 0
            prev_store = ev["store"][i - 2] if i >= 2 else 0

            @blk.sync
            def _(sync):
                if prev_apply:
                    sync.wait_ge(eS.h, prev_apply)
                ev["load"].append(qS.dinc(
                    sync.dma_start(out=xbuf[:, p, :], in_=src_ap[r0:r0 + 128, :])))

            myload = ev["load"][i]

            @blk.vector
            def _(vector):
                vector.wait_ge(qS.h, myload)
                if prev_apply:
                    vector.wait_ge(eS.h, prev_apply)
                for c in range(4):
                    ins = vector.bn_stats(bns[:, p, c, :], xbuf[:, p, c * 512:(c + 1) * 512])
                eV.inc(ins)
                vector.wait_ge(eV.h, eV.n)
                eV.inc(vector.bn_aggr(bna[:, p, :], bns[:, p, :, :]))
                vector.wait_ge(eV.h, eV.n)
                eV.inc(vector.tensor_scalar(rstd[:, p, :], bna[:, p, 1:2], 1.0, EPS,
                                            ALU.mult, ALU.add))
                vector.wait_ge(eV.h, eV.n)
                eV.inc(vector.reciprocal(rstd[:, p, :], rstd[:, p, :]))

            v1 = eV.n

            @blk.scalar
            def _(scalar):
                scalar.wait_ge(eV.h, v1)
                eS.inc(scalar.activation(rstd[:, p, :], rstd[:, p, :], AF.Sqrt))

            s1 = eS.n

            @blk.vector
            def _(vector):
                vector.wait_ge(eS.h, s1)
                eV.inc(vector.tensor_scalar(nbuf[:, p, :], bna[:, p, 0:1], rstd[:, p, 0:1],
                                            -1.0, ALU.mult, ALU.mult))

            v2 = eV.n

            @blk.scalar
            def _(scalar):
                scalar.wait_ge(eV.h, v2)
                if prev_store:
                    scalar.wait_ge(qS.h, prev_store)
                ev["apply"].append(eS.inc(
                    scalar.activation(xnb[:, p, :], xbuf[:, p, :], AF.Identity,
                                      bias=nbuf[:, p, :], scale=rstd[:, p, :])))

            myapply = ev["apply"][i]

            @blk.sync
            def _(sync):
                sync.wait_ge(eS.h, myapply)
                ev["store"].append(qS.dinc(
                    sync.dma_start(out=dst_scr[r0:r0 + 128, :], in_=xnb[:, p, :])))

        def phase1():
          with ExitStack() as es:
            xbuf = es.enter_context(nc.sbuf_tensor("xbuf", [128, 4, M], F32))
            xnb = es.enter_context(nc.sbuf_tensor("xnb", [128, 2, M], BF16))
            bns = es.enter_context(nc.sbuf_tensor("bns", [128, 4, 4, 6], F32))
            bna = es.enter_context(nc.sbuf_tensor("bna", [128, 4, 2], F32))
            lnt = es.enter_context(nc.sbuf_tensor("lnt", [128, 4, 1], F32))
            rsall = es.enter_context(nc.sbuf_tensor("rsall", [128, 80], F32))
            xnT = es.enter_context(nc.sbuf_tensor("xnT", [128, 2, NMC, 1024], BF16))

            jobs = [(xownd, xnoscr, t) for t in range(NOWN)] + \
                   [(xbd, xnscr, t) for t in range(64)]
            NJ = len(jobs)
            loadv = [0] * NJ
            bnev = [0] * NJ
            agev = [0] * NJ
            rsallv = [0] * NJ
            applyv = [0] * NJ
            storev = [0] * NJ

            def ln_load(i):
                src_ap, _, tau = jobs[i]
                p = i % 4

                @blk.sync
                def _(sync):
                    if i >= 4:
                        sync.wait_ge(eV.h, applyv[i - 4])
                    loadv[i] = qS.dinc(sync.dma_start(
                        out=xbuf[:, p, :], in_=src_ap[tau * 128:tau * 128 + 128, :]))

            def ln_A(i):
                p = i % 4

                @blk.vector
                def _(vector):
                    vector.wait_ge(qS.h, loadv[i])
                    if i >= 4:
                        vector.wait_ge(eV.h, applyv[i - 4])
                    for c in range(4):
                        ins = vector.bn_stats(bns[:, p, c, :],
                                              xbuf[:, p, c * 512:(c + 1) * 512])
                    bnev[i] = eV.inc(ins)

            def ln_B(i):
                p = i % 4

                @blk.vector
                def _(vector):
                    vector.wait_ge(eV.h, bnev[i])
                    agev[i] = eV.inc(vector.bn_aggr(bna[:, p, :], bns[:, p, :, :]))

            def ln_S(i):
                p = i % 4

                @blk.scalar
                def _(scalar):
                    scalar.wait_ge(eV.h, agev[i])
                    eS.inc(scalar.activation(lnt[:, p, :], bna[:, p, 1:2], AF.Ln,
                                             bias=ceps[:, 0:1]))
                    scalar.wait_ge(eS.h, eS.n)
                    rsallv[i] = eS.inc(scalar.activation(rsall[:, i:i + 1],
                                                         lnt[:, p, :], AF.Exp,
                                                         scale=-0.5))

            def ln_C(i):
                _, dst_scr, tau = jobs[i]
                p = i % 4

                @blk.vector
                def _(vector):
                    vector.wait_ge(eV.h, agev[i])
                    if i >= 2:
                        vector.wait_ge(qA.h, storev[i - 2])
                    applyv[i] = eV.inc(vector.tensor_scalar(
                        xnb[:, i % 2, :], xbuf[:, p, :], bna[:, p, 0:1], None,
                        ALU.subtract))

                @blk.scalar
                def _(scalar):
                    scalar.wait_ge(eV.h, applyv[i])
                    storev[i] = qA.dinc(scalar.dma_start(
                        out=dst_scr[tau * 128:tau * 128 + 128, :], in_=xnb[:, i % 2, :]))

            def ln_run(lo, hi):
                for i in range(lo, hi + 1):
                    if i < hi:
                        ln_load(i)
                        ln_A(i)
                    if lo <= i - 1 < hi:
                        ln_B(i - 1)
                        ln_S(i - 1)
                        ln_C(i - 1)

            ln_run(0, 8)

            qmm, qropev, qstev, cqev, trl2ev = [], [], [], [], []

            with ExitStack() as esb:
                wq_sb = esb.enter_context(nc.sbuf_tensor("wq_sb", [128, NMC, 2048], BF16))
                cqb = esb.enter_context(nc.sbuf_tensor("cqb", [128, 2, G * H, 64], F32))
                sqb = esb.enter_context(nc.sbuf_tensor("sqb", [128, 2, G * H, 64], F32))
                qtmp = esb.enter_context(nc.sbuf_tensor("qtmp", [128, 2, G * H, 64], F32))
                qbuf = esb.enter_context(nc.sbuf_tensor("qbuf", [128, 2, 2048], BF16))
                qps = esb.enter_context(nc.psum_tensor("qps", [128, 2, 2048], F32))

                @blk.gpsimd
                def _(gpsimd):
                    qG.dinc(gpsimd.dma_start(
                        out=wq_sb[:, :, :],
                        in_=wqd.rearrange("(mc p) n -> p mc n", p=128)))

                wq_ev = qG.n

                for OS in range(2):
                    store_upto = storev[8 * OS + 7]

                    @blk.scalar
                    def _(scalar):
                        scalar.wait_ge(qA.h, store_upto)
                        for mc in range(NMC):
                            qA.dinc(scalar.dma_start(
                                out=xnT[:, OS % 2, mc, :],
                                in_=xnoscr[8 * OS * 128:(8 * OS + 8) * 128,
                                           mc * 128:(mc + 1) * 128],
                                transpose=True))

                    trl2ev.append(qA.n)
                    if OS == 0:
                        ln_run(8, NOWN)

                    for t8 in range(8):
                        ti = 8 * OS + t8
                        pq = ti % 2

                        @blk.tensor
                        def _(tensor):
                            tensor.wait_ge(qA.h, trl2ev[OS])
                            if ti == 0:
                                tensor.wait_ge(qG.h, wq_ev)
                            if ti >= 2:
                                tensor.wait_ge(eV.h, qropev[ti - 2])
                            for mc in range(NMC):
                                for q4 in range(4):
                                    ins = tensor.matmul(
                                        qps[:, pq, q4 * 512:(q4 + 1) * 512],
                                        xnT[:, OS % 2, mc, t8 * 128:(t8 + 1) * 128],
                                        wq_sb[:, mc, q4 * 512:(q4 + 1) * 512],
                                        start=(mc == 0), stop=(mc == NMC - 1))
                            eT.inc(ins)

                        qmm.append(eT.n)

                        @blk.sync
                        def _(sync):
                            if ti >= 2:
                                sync.wait_ge(eV.h, qropev[ti - 2])
                            qS.dinc(sync.dma_start(
                                out=cqb[:, pq, :, :],
                                in_=cqd[ti * 128:(ti + 1) * 128, :].rearrange(
                                    "p (h f) -> p h f", h=G * H)))
                            qS.dinc(sync.dma_start(
                                out=sqb[:, pq, :, :],
                                in_=sqd[ti * 128:(ti + 1) * 128, :].rearrange(
                                    "p (h f) -> p h f", h=G * H)))

                        cqev.append(qS.n)

                        @blk.vector
                        def _(vector):
                            vector.wait_ge(eT.h, qmm[ti])
                            vector.wait_ge(qS.h, cqev[ti])
                            vector.wait_ge(eS.h, rsallv[ti])
                            if ti >= 2:
                                vector.wait_ge(qG.h, qstev[ti - 2])
                            rs = rsall[:, ti:ti + 1]
                            qview = qps[:, pq, :].rearrange("p (h t f) -> p h t f",
                                                            h=G * H, t=2)
                            qout = qbuf[:, pq, :].rearrange("p (h t f) -> p h t f",
                                                            h=G * H, t=2)
                            qe, qo = qview[:, :, 0, :], qview[:, :, 1, :]
                            vector.scalar_tensor_tensor(qtmp[:, 0, :, :], qe, rs,
                                                        cqb[:, pq, :, :], ALU.mult, ALU.mult)
                            vector.scalar_tensor_tensor(qtmp[:, 1, :, :], qo, rs,
                                                        sqb[:, pq, :, :], ALU.mult, ALU.mult)
                            vector.tensor_tensor(qout[:, :, 0, :], qtmp[:, 0, :, :],
                                                 qtmp[:, 1, :, :], ALU.subtract)
                            vector.scalar_tensor_tensor(qtmp[:, 0, :, :], qe, rs,
                                                        sqb[:, pq, :, :], ALU.mult, ALU.mult)
                            vector.scalar_tensor_tensor(qtmp[:, 1, :, :], qo, rs,
                                                        cqb[:, pq, :, :], ALU.mult, ALU.mult)
                            ins = vector.tensor_tensor(qout[:, :, 1, :], qtmp[:, 0, :, :],
                                                       qtmp[:, 1, :, :], ALU.add)
                            eV.inc(ins)

                        qropev.append(eV.n)

                        @blk.gpsimd
                        def _(gpsimd):
                            gpsimd.wait_ge(eV.h, qropev[ti])
                            qstev.append(qG.dinc(gpsimd.dma_start(
                                out=qscr[ti * 128:(ti + 1) * 128, :],
                                in_=qbuf[:, pq, :])))

            q_done = (qmm[-1], qropev[-1], qstev[-1])
            with ExitStack() as esa:
                wkv_sb = esa.enter_context(nc.sbuf_tensor("wkv_sb", [128, NMC, 1024], BF16))
                ckb = esa.enter_context(nc.sbuf_tensor("ckb", [128, 2, G, 64], F32))
                skb = esa.enter_context(nc.sbuf_tensor("skb", [128, 2, G, 64], F32))
                rtmp = esa.enter_context(nc.sbuf_tensor("rtmp", [128, 2, G, 64], F32))
                kvbuf = esa.enter_context(nc.sbuf_tensor("kvbuf", [128, 2, 1024], BF16))
                kvps = esa.enter_context(nc.psum_tensor("kvps", [128, 2, 1024], F32))

                @blk.gpsimd
                def _(gpsimd):
                    gpsimd.wait_ge(eT.h, q_done[0])
                    gpsimd.wait_ge(eV.h, q_done[1])
                    gpsimd.wait_ge(qG.h, q_done[2])
                    qG.dinc(gpsimd.dma_start(
                        out=wkv_sb[:, :, :],
                        in_=wkvd.rearrange("(mc p) n -> p mc n", p=128)))

                wkv_ev = qG.n
                kvmm, ropev, vcpev, kvstev, ckskev, trlev = [], [], [], [], [], []

                ln_run(NOWN, NOWN + 8)
                for S in range(8):
                    store_upto = storev[NOWN + 8 * S + 7]

                    @blk.scalar
                    def _(scalar):
                        scalar.wait_ge(qA.h, store_upto)
                        if S >= 2:
                            scalar.wait_ge(eT.h, kvmm[8 * (S - 2) + 7])
                        else:
                            scalar.wait_ge(eT.h, q_done[0])
                        for mc in range(NMC):
                            qA.dinc(scalar.dma_start(
                                out=xnT[:, S % 2, mc, :],
                                in_=xnscr[8 * S * 128:(8 * S + 8) * 128,
                                          mc * 128:(mc + 1) * 128],
                                transpose=True))

                    trlev.append(qA.n)
                    if S + 1 < 8:
                        ln_run(NOWN + 8 * (S + 1), NOWN + 8 * (S + 1) + 8)

                    for t8 in range(8):
                        tau = 8 * S + t8
                        pk = tau % 2

                        @blk.tensor
                        def _(tensor):
                            tensor.wait_ge(qA.h, trlev[S])
                            if tau == 0:
                                tensor.wait_ge(qG.h, wkv_ev)
                            if tau >= 2:
                                tensor.wait_ge(eV.h, ropev[tau - 2])
                            for mc in range(NMC):
                                tensor.matmul(kvps[:, pk, 0:512],
                                              xnT[:, S % 2, mc, t8 * 128:(t8 + 1) * 128],
                                              wkv_sb[:, mc, 0:512],
                                              start=(mc == 0), stop=(mc == NMC - 1))
                                ins = tensor.matmul(kvps[:, pk, 512:1024],
                                                    xnT[:, S % 2, mc, t8 * 128:(t8 + 1) * 128],
                                                    wkv_sb[:, mc, 512:1024],
                                                    start=(mc == 0), stop=(mc == NMC - 1))
                            eT.inc(ins)

                        kvmm.append(eT.n)

                        @blk.sync
                        def _(sync):
                            if tau >= 2:
                                sync.wait_ge(eV.h, ropev[tau - 2])
                            qS.dinc(sync.dma_start(
                                out=ckb[:, pk, :, :],
                                in_=ckd[tau * 128:(tau + 1) * 128, :].rearrange(
                                    "p (g f) -> p g f", g=G)))
                            qS.dinc(sync.dma_start(
                                out=skb[:, pk, :, :],
                                in_=skd[tau * 128:(tau + 1) * 128, :].rearrange(
                                    "p (g f) -> p g f", g=G)))

                        ckskev.append(qS.n)

                        @blk.vector
                        def _(vector):
                            vector.wait_ge(eT.h, kvmm[tau])
                            vector.wait_ge(qS.h, ckskev[tau])
                            vector.wait_ge(eS.h, rsallv[NOWN + tau])
                            if tau >= 2:
                                vector.wait_ge(qG.h, kvstev[tau - 2])
                            rs = rsall[:, NOWN + tau:NOWN + tau + 1]
                            kview = kvps[:, pk, 0:512].rearrange("p (g t f) -> p g t f",
                                                                 g=G, t=2)
                            kout = kvbuf[:, pk, 0:512].rearrange("p (g t f) -> p g t f",
                                                                 g=G, t=2)
                            ke, ko = kview[:, :, 0, :], kview[:, :, 1, :]
                            vector.scalar_tensor_tensor(rtmp[:, 0, :, :], ke, rs,
                                                        ckb[:, pk, :, :], ALU.mult, ALU.mult)
                            vector.scalar_tensor_tensor(rtmp[:, 1, :, :], ko, rs,
                                                        skb[:, pk, :, :], ALU.mult, ALU.mult)
                            vector.tensor_tensor(kout[:, :, 0, :], rtmp[:, 0, :, :],
                                                 rtmp[:, 1, :, :], ALU.subtract)
                            vector.scalar_tensor_tensor(rtmp[:, 0, :, :], ke, rs,
                                                        skb[:, pk, :, :], ALU.mult, ALU.mult)
                            vector.scalar_tensor_tensor(rtmp[:, 1, :, :], ko, rs,
                                                        ckb[:, pk, :, :], ALU.mult, ALU.mult)
                            vector.tensor_tensor(kout[:, :, 1, :], rtmp[:, 0, :, :],
                                                 rtmp[:, 1, :, :], ALU.add)
                            ins = vector.tensor_scalar(kvbuf[:, pk, 512:1024],
                                                       kvps[:, pk, 512:1024], rs, None,
                                                       ALU.mult)
                            eV.inc(ins)

                        ropev.append(eV.n)

                        @blk.gpsimd
                        def _(gpsimd):
                            gpsimd.wait_ge(eV.h, ropev[tau])
                            kvstev.append(qG.dinc(gpsimd.dma_start(
                                out=kvscr[tau * 128:(tau + 1) * 128, :],
                                in_=kvbuf[:, pk, :])))

        if phases >= 1:
            phase1()

        barrier()
        oT_ctx = nc.sbuf_tensor("oT", [128, G * H, 2048], BF16)
        oT = oT_ctx.__enter__()

        def phase2():
          with ExitStack() as es:
            kT = es.enter_context(nc.sbuf_tensor("kT", [128, 2, T], BF16))
            vG = es.enter_context(nc.sbuf_tensor("vG", [128, 2, 64, 128], BF16))
            qT = es.enter_context(nc.sbuf_tensor("qT", [128, 2, H, 2048], BF16))
            maskb = es.enter_context(nc.sbuf_tensor("maskb", [128, 2, 8, 256], BF16))
            ptile = es.enter_context(nc.sbuf_tensor("ptile", [128, 2, H, 256], BF16))
            zln = es.enter_context(nc.sbuf_tensor("zln", [128, 1024], F32))
            zrec = es.enter_context(nc.sbuf_tensor("zrec", [128, 1024], F32))
            sps = es.enter_context(nc.psum_tensor("sps", [128, 2, 1024], F32))
            oacc = es.enter_context(nc.psum_tensor("oacc", [128, 1024], F32))
            zacc = es.enter_context(nc.psum_tensor("zacc", [128, 1024], F32))
            if True:

                gldev = []
                gmmdone = [0] * G
                onormev = {}
                zrecev = {}
                msk_last = {}

                def emit_group_loads(g):
                    gp = g % 2

                    @blk.sync
                    def _(sync):
                        if g >= 2:
                            sync.wait_ge(eT.h, gmmdone[g - 2])
                        qS.dinc(sync.dma_start(
                            out=kT[:, gp, :],
                            in_=kvscr[:, g * 128:(g + 1) * 128], transpose=True))
                        for h in range(H):
                            qS.dinc(sync.dma_start(
                                out=qT[:, gp, h, :],
                                in_=qscr[:, (g * H + h) * 128:(g * H + h + 1) * 128],
                                transpose=True))
                        qS.dinc(sync.dma_start(
                            out=vG[:, gp, :, :],
                            in_=kvscr[:, 512 + g * 128:512 + (g + 1) * 128].rearrange(
                                "(cc p) d -> p cc d", p=128)))

                    gldev.append((qS.n, qS.n, qS.n))

                emit_group_loads(0)
                emit_group_loads(1)

                for g in range(G):
                    gp = g % 2
                    for sp in range(NSP):
                        L = 8 * sp + 8
                        qsl = slice(sp * 256, (sp + 1) * 256)
                        T0 = eT.n
                        S0 = eS.n
                        V0 = eV.n

                        def f_sdone(mc):
                            return T0 + (1 if mc == 0 else 2 * mc)

                        def f_ozdone(mc):
                            return T0 + (2 * L if mc == L - 1 else 2 * mc + 3)

                        def f_mskev(mc):
                            return V0 + (mc - 8 * sp) + 1

                        @blk.sync
                        def _(sync):
                            prevkey = (g, sp - 2) if sp >= 2 else (g - 1, sp + 6)
                            prev = msk_last.get(prevkey, 0)
                            if prev:
                                sync.wait_ge(eV.h, prev)
                            qS.dinc(sync.dma_start(
                                out=maskb[:, sp % 2, :, :],
                                in_=maskd[sp, :, :, :].rearrange("m p i -> p m i")))

                        mskld = qS.n
                        prevkey = (g, sp - 1) if sp > 0 else (g - 1, NSP - 1)

                        @blk.scalar
                        def _(scalar):
                            for mc in range(L):
                                scalar.wait_ge(eT.h, f_sdone(mc))
                                eS.inc(scalar.activation(
                                    ptile[:, mc % 2, :, :],
                                    sps[:, mc % 2, :].rearrange("p (h i) -> p h i", h=H),
                                    AF.Exp, bias=cbias[:, 0:1]))
                            scalar.wait_ge(eT.h, T0 + 2 * L)
                            if prevkey in onormev:
                                scalar.wait_ge(eV.h, onormev[prevkey])
                            eS.inc(scalar.activation(zln[:, :], zacc[:, :], AF.Ln))
                            scalar.wait_ge(eS.h, eS.n)
                            eS.inc(scalar.activation(zrec[:, :], zln[:, :], AF.Exp,
                                                     scale=-1.0))

                        zrecev[(g, sp)] = eS.n
                        assert eS.n == S0 + L + 2

                        @blk.tensor
                        def _(tensor):
                            def emit_s(mc):
                                if mc == 0:
                                    tensor.wait_ge(qS.h, gldev[g][0])
                                    if prevkey in onormev:
                                        tensor.wait_ge(eV.h, onormev[prevkey])
                                        tensor.wait_ge(eS.h, zrecev[prevkey])
                                if mc >= 2:
                                    tensor.wait_ge(eS.h, S0 + mc - 1)
                                tensor.matmul(sps[:, mc % 2, 0:512],
                                              kT[:, gp, mc * 128:(mc + 1) * 128],
                                              qT[:, gp, 0:2, qsl],
                                              start=True, stop=True)
                                ins = tensor.matmul(sps[:, mc % 2, 512:1024],
                                                    kT[:, gp, mc * 128:(mc + 1) * 128],
                                                    qT[:, gp, 2:4, qsl],
                                                    start=True, stop=True)
                                eT.inc(ins)
                                assert eT.n == f_sdone(mc)

                            def emit_oz(mc):
                                tensor.wait_ge(eS.h, S0 + mc + 1)
                                if mc >= 8 * sp:
                                    tensor.wait_ge(eV.h, f_mskev(mc))
                                    if mc == 8 * sp:
                                        tensor.wait_ge(qS.h, mskld)
                                st_ = (mc == 0)
                                fin = (mc == L - 1)
                                par = mc % 2
                                tensor.matmul(oacc[:, 0:512], vG[:, gp, mc, :],
                                              ptile[:, par, 0:2, :], start=st_, stop=fin)
                                tensor.matmul(oacc[:, 512:1024], vG[:, gp, mc, :],
                                              ptile[:, par, 2:4, :], start=st_, stop=fin)
                                tensor.matmul(zacc[:, 0:512], ones_sb[:, :],
                                              ptile[:, par, 0:2, :], start=st_, stop=fin)
                                ins = tensor.matmul(zacc[:, 512:1024], ones_sb[:, :],
                                                    ptile[:, par, 2:4, :], start=st_, stop=fin)
                                eT.inc(ins)
                                assert eT.n == f_ozdone(mc)

                            for mc in range(L):
                                emit_s(mc)
                                if mc >= 1:
                                    emit_oz(mc - 1)
                            emit_oz(L - 1)

                        @blk.vector
                        def _(vector):
                            for mc in range(8 * sp, L):
                                mcl = mc - 8 * sp
                                vector.wait_ge(eS.h, S0 + mc + 1)
                                if mc == 8 * sp:
                                    vector.wait_ge(qS.h, mskld)
                                mb = maskb[:, sp % 2, mcl, :].unsqueeze(1) \
                                    .broadcast_to((128, H, 256))
                                ins = vector.tensor_tensor(ptile[:, mc % 2, :, :],
                                                           ptile[:, mc % 2, :, :],
                                                           mb, ALU.mult)
                                eV.inc(ins)
                                assert eV.n == f_mskev(mc)
                            vector.wait_ge(eS.h, zrecev[(g, sp)])
                            vector.wait_ge(eT.h, T0 + 2 * L)
                            ins = vector.tensor_tensor(
                                oT[:, g * H:(g + 1) * H, qsl],
                                oacc[:, :].rearrange("p (h i) -> p h i", h=H),
                                zrec[:, :].rearrange("p (h i) -> p h i", h=H),
                                ALU.mult)
                            eV.inc(ins)

                        msk_last[(g, sp)] = V0 + 8
                        onormev[(g, sp)] = eV.n

                    gmmdone[g] = eT.n
                    if g + 2 < G:
                        emit_group_loads(g + 2)

        phase2()

        barrier()

        def phase3():
          with ExitStack() as es:
            wao_sb = es.enter_context(nc.sbuf_tensor("wao_sb", [128, G * H, 2048], BF16))
            xob = es.enter_context(nc.sbuf_tensor("xob", [128, 2, 2048], F32))
            xmb = es.enter_context(nc.sbuf_tensor("xmb", [128, 2, 2048], F32))
            xn2b = es.enter_context(nc.sbuf_tensor("xn2b", [128, 2, 2048], BF16))
            bns3 = es.enter_context(nc.sbuf_tensor("bns3", [128, 2, 4, 6], F32))
            bna3 = es.enter_context(nc.sbuf_tensor("bna3", [128, 2, 2], F32))
            rstd3 = es.enter_context(nc.sbuf_tensor("rstd3", [128, 2, 1], F32))
            nb3 = es.enter_context(nc.sbuf_tensor("nb3", [128, 2, 1], F32))
            rps = es.enter_context(nc.psum_tensor("rps", [128, 2, 2048], F32))
            if True:

                waoev = []
                for gh in range(G * H):
                    @blk.gpsimd
                    def _(gpsimd):
                        qG.dinc(gpsimd.dma_start(out=wao_sb[:, gh, :],
                                                 in_=waod[gh, :, :]))
                    waoev.append(qG.n)

                rmm, xmidv, xldev, xn2ev, p3stev = [], [], [], [], []

                for t in range(NOWN):
                    pr = t % 2

                    @blk.tensor
                    def _(tensor):
                        tensor.wait_ge(qG.h, waoev[-1])
                        if t >= 2:
                            tensor.wait_ge(eV.h, xmidv[t - 2])
                        for gh in range(G * H):
                            for q4 in range(4):
                                ins = tensor.matmul(
                                    rps[:, pr, q4 * 512:(q4 + 1) * 512],
                                    oT[:, gh, t * 128:(t + 1) * 128],
                                    wao_sb[:, gh, q4 * 512:(q4 + 1) * 512],
                                    start=(gh == 0), stop=(gh == G * H - 1))
                        eT.inc(ins)

                    rmm.append(eT.n)

                    @blk.sync
                    def _(sync):
                        if t >= 2:
                            sync.wait_ge(eV.h, xmidv[t - 2])
                        qS.dinc(sync.dma_start(out=xob[:, pr, :],
                                               in_=xownd[t * 128:(t + 1) * 128, :]))

                    xldev.append(qS.n)

                    @blk.vector
                    def _(vector):
                        vector.wait_ge(eT.h, rmm[t])
                        vector.wait_ge(qS.h, xldev[t])
                        if t >= 2:
                            vector.wait_ge(qS.h, p3stev[t - 2])
                        eV.inc(vector.tensor_add(xmb[:, pr, :], rps[:, pr, :],
                                                 xob[:, pr, :]))
                        vector.wait_ge(eV.h, eV.n)
                        for c in range(4):
                            ins = vector.bn_stats(bns3[:, pr, c, :],
                                                  xmb[:, pr, c * 512:(c + 1) * 512])
                        eV.inc(ins)
                        vector.wait_ge(eV.h, eV.n)
                        eV.inc(vector.bn_aggr(bna3[:, pr, :], bns3[:, pr, :, :]))
                        vector.wait_ge(eV.h, eV.n)
                        eV.inc(vector.tensor_scalar(rstd3[:, pr, :], bna3[:, pr, 1:2],
                                                    1.0, EPS, ALU.mult, ALU.add))
                        vector.wait_ge(eV.h, eV.n)
                        eV.inc(vector.reciprocal(rstd3[:, pr, :], rstd3[:, pr, :]))

                    xmidv.append(eV.n)
                    vv = eV.n

                    @blk.scalar
                    def _(scalar):
                        scalar.wait_ge(eV.h, vv)
                        eS.inc(scalar.activation(rstd3[:, pr, :], rstd3[:, pr, :], AF.Sqrt))

                    ss = eS.n

                    @blk.vector
                    def _(vector):
                        vector.wait_ge(eS.h, ss)
                        eV.inc(vector.tensor_scalar(nb3[:, pr, :], bna3[:, pr, 0:1],
                                                    rstd3[:, pr, 0:1], -1.0,
                                                    ALU.mult, ALU.mult))

                    vv2 = eV.n

                    @blk.scalar
                    def _(scalar):
                        scalar.wait_ge(eV.h, vv2)
                        if t >= 2:
                            scalar.wait_ge(qS.h, p3stev[t - 2])
                        eS.inc(scalar.activation(xn2b[:, pr, :], xmb[:, pr, :],
                                                 AF.Identity, bias=nb3[:, pr, :],
                                                 scale=rstd3[:, pr, :]))

                    xn2ev.append(eS.n)

                    @blk.sync
                    def _(sync):
                        sync.wait_ge(eS.h, xn2ev[t])
                        qS.dinc(sync.dma_start(out=xmidscr[t * 128:(t + 1) * 128, :],
                                               in_=xmb[:, pr, :]))
                        qS.dinc(sync.dma_start(out=xn2scr[t * 128:(t + 1) * 128, :],
                                               in_=xn2b[:, pr, :]))

                    p3stev.append(qS.n)

        if phases >= 3:
            phase3()
        oT_ctx.__exit__(None, None, None)

        barrier()

        def phase4():
          slot_scope = ExitStack()
          with slot_scope:
            slot1i = slot_scope.enter_context(nc.sbuf_tensor("slot1i", [128, 16], I32))
            slot2i = slot_scope.enter_context(nc.sbuf_tensor("slot2i", [128, 16], I32))
            c1all = slot_scope.enter_context(nc.sbuf_tensor("c1all", [128, 16], F32))
            c2all = slot_scope.enter_context(nc.sbuf_tensor("c2all", [128, 16], F32))
            with ExitStack() as es:
                xn2T = es.enter_context(nc.sbuf_tensor("xn2T", [128, NMC, 2048], BF16))
                gwsb = es.enter_context(nc.sbuf_tensor("gwsb", [128, NMC, 8], BF16))
                comb = es.enter_context(nc.sbuf_tensor("comb", [128, 16, 8], F32))
                m1b = es.enter_context(nc.sbuf_tensor("m1b", [128, 16, 8], BF16))
                cmb = es.enter_context(nc.sbuf_tensor("cmb", [128, 16, 8], BF16))
                idf = es.enter_context(nc.sbuf_tensor("idf", [128, 128], F32))
                idb = es.enter_context(nc.sbuf_tensor("idb", [128, 128], BF16))
                eoff = es.enter_context(nc.sbuf_tensor("eoff", [128, 8], F32))
                cmT = es.enter_context(nc.sbuf_tensor("cmT", [8, 2048], F32))
                cmT2 = es.enter_context(nc.sbuf_tensor("cmT2", [8, 2048], F32))
                m1T = es.enter_context(nc.sbuf_tensor("m1T", [8, 2048], F32))
                posT = es.enter_context(nc.sbuf_tensor("posT", [128, 16, 8], F32))
                flatm = es.enter_context(nc.sbuf_tensor("flatm", [128, 8], F32))
                ovf = es.enter_context(nc.sbuf_tensor("ovf", [128, 8], F32))
                m2tk = es.enter_context(nc.sbuf_tensor("m2tk", [128, 8], F32))
                sltf = es.enter_context(nc.sbuf_tensor("sltf", [128, 2, 16], F32))
                tp1 = es.enter_context(nc.sbuf_tensor("tp1", [128, 8], F32))
                tp2 = es.enter_context(nc.sbuf_tensor("tp2", [128, 8], F32))
                tl1 = es.enter_context(nc.sbuf_tensor("tl1", [128, 1], F32))
                tl2 = es.enter_context(nc.sbuf_tensor("tl2", [128, 1], F32))
                tc1 = es.enter_context(nc.sbuf_tensor("tc1", [128, 1], F32))
                tc2 = es.enter_context(nc.sbuf_tensor("tc2", [128, 1], F32))
                xn2tl = es.enter_context(nc.sbuf_tensor("xn2tl", [128, 2, M], BF16))
                lgps = es.enter_context(nc.psum_tensor("p4lg", [128, 512], F32))
                trps = es.enter_context(nc.psum_tensor("p4tr", [128, 2, 1024], BF16))
                trpf = es.enter_context(nc.psum_tensor("p4trf", [128, 2, 512], F32))

                @blk.gpsimd
                def _(gpsimd):
                    qG.dinc(gpsimd.dma_start(
                        out=gwsb[:, :, :],
                        in_=gwd.rearrange("(mc p) e -> p mc e", p=128)))
                    qG.dinc(gpsimd.dma_start(out=idf[:, :], in_=idf32d[:, :]))
                    qG.dinc(gpsimd.dma_start(out=idb[:, :], in_=idf32d[:, :]))
                    qG.dinc(gpsimd.dma_start(out=eoff[:, :], in_=eoffd[:, :]))

                cons_ev = qG.n

                @blk.sync
                def _(sync):
                    for mc in range(NMC):
                        qA.dinc(sync.dma_start(
                            out=xn2T[:, mc, :],
                            in_=xn2scr[:, mc * 128:(mc + 1) * 128],
                            transpose=True))

                xn2Tev = qA.n

                lgcpev = []
                for t in range(NOWN):
                    @blk.tensor
                    def _(tensor):
                        tensor.wait_ge(qA.h, xn2Tev)
                        if t == 0:
                            tensor.wait_ge(qG.h, cons_ev)
                        else:
                            tensor.wait_ge(eV.h, lgcpev[t - 1])
                        for mc in range(NMC):
                            ins = tensor.matmul(lgps[:, 0:8],
                                                xn2T[:, mc, t * 128:(t + 1) * 128],
                                                gwsb[:, mc, :],
                                                start=(mc == 0), stop=(mc == NMC - 1))
                        eT.inc(ins)

                    lgev = eT.n

                    @blk.vector
                    def _(vector):
                        vector.wait_ge(eT.h, lgev)
                        eV.inc(vector.tensor_copy(tp1[:, :], lgps[:, 0:8]))
                        lgcpev.append(eV.n)
                        vector.wait_ge(eV.h, eV.n)
                        eV.inc(vector.reduce_max(tl1[:, :], tp1[:, :], axis=AX.X))
                        vector.wait_ge(eV.h, eV.n)
                        eV.inc(vector.tensor_scalar(comb[:, t, :], tp1[:, :],
                                                    tl1[:, 0:1], None, ALU.is_equal))
                        vector.wait_ge(eV.h, eV.n)
                        eV.inc(vector.tensor_copy(m1b[:, t, :], comb[:, t, :]))
                        eV.inc(vector.scalar_tensor_tensor(tp2[:, :], comb[:, t, :],
                                                           -1e30, tp1[:, :],
                                                           ALU.mult, ALU.add))
                        vector.wait_ge(eV.h, eV.n)
                        eV.inc(vector.reduce_max(tl2[:, :], tp2[:, :], axis=AX.X))
                        vector.wait_ge(eV.h, eV.n)
                        eV.inc(vector.tensor_scalar(tp2[:, :], tp2[:, :], tl2[:, 0:1],
                                                    None, ALU.is_equal))
                        vector.wait_ge(eV.h, eV.n)
                        eV.inc(vector.tensor_scalar(tc1[:, :], tl1[:, :], tl2[:, 0:1],
                                                    None, ALU.subtract))

                    vge = eV.n

                    @blk.scalar
                    def _(scalar):
                        scalar.wait_ge(eV.h, vge)
                        eS.inc(scalar.activation(tc1[:, :], tc1[:, :], AF.Sigmoid))

                    sge = eS.n

                    @blk.vector
                    def _(vector):
                        vector.wait_ge(eS.h, sge)
                        eV.inc(vector.tensor_scalar(tc2[:, :], tc1[:, :], -1.0, 1.0,
                                                    ALU.mult, ALU.add))
                        vector.wait_ge(eV.h, eV.n)
                        eV.inc(vector.tensor_copy(c1all[:, t:t + 1], tc1[:, :]))
                        eV.inc(vector.tensor_copy(c2all[:, t:t + 1], tc2[:, :]))
                        eV.inc(vector.tensor_scalar(comb[:, t, :], comb[:, t, :],
                                                    tc1[:, 0:1], None, ALU.mult))
                        vector.wait_ge(eV.h, eV.n)
                        eV.inc(vector.scalar_tensor_tensor(comb[:, t, :], tp2[:, :],
                                                           tc2[:, 0:1], comb[:, t, :],
                                                           ALU.mult, ALU.add))
                        vector.wait_ge(eV.h, eV.n)
                        eV.inc(vector.tensor_scalar(cmb[:, t, :], comb[:, t, :], 0.0,
                                                    None, ALU.is_gt))

                combev = eV.n

                trev = []
                for t in range(NOWN):
                    @blk.tensor
                    def _(tensor):
                        if t == 0:
                            tensor.wait_ge(eV.h, combev)
                        else:
                            tensor.wait_ge(eV.h, trev[t - 1])
                        tensor.transpose(trps[0:8, 0, 0:128], cmb[:, t, :], idb[:, :])
                        ins = tensor.transpose(trps[0:8, 1, 0:128], m1b[:, t, :], idb[:, :])
                        eT.inc(ins)

                    myev = eT.n

                    @blk.vector
                    def _(vector):
                        vector.wait_ge(eT.h, myev)
                        vector.tensor_copy(cmT[0:8, t * 128:(t + 1) * 128],
                                           trps[0:8, 0, 0:128])
                        ins = vector.tensor_copy(m1T[0:8, t * 128:(t + 1) * 128],
                                                 trps[0:8, 1, 0:128])
                        eV.inc(ins)

                    trev.append(eV.n)

                @blk.vector
                def _(vector):
                    vector.wait_ge(eV.h, trev[-1])
                    bufs = [cmT, cmT2]
                    k = 1
                    while k < 2048:
                        a, bnxt = bufs[0], bufs[1]
                        vector.wait_ge(eV.h, eV.n)
                        eV.inc(vector.tensor_copy(bnxt[0:8, 0:k], a[0:8, 0:k]))
                        eV.inc(vector.tensor_add(bnxt[0:8, k:2048], a[0:8, k:2048],
                                                 a[0:8, 0:2048 - k]))
                        bufs = [bnxt, a]
                        k *= 2
                    vector.wait_ge(eV.h, eV.n)
                    eV.inc(vector.tensor_copy(cmT[0:8, :], bufs[0][0:8, :]))

                cumev = eV.n

                sltev = []
                for t in range(NOWN):
                    @blk.tensor
                    def _(tensor):
                        if t == 0:
                            tensor.wait_ge(eV.h, cumev)
                            tensor.wait_ge(qG.h, cons_ev)
                        else:
                            tensor.wait_ge(eV.h, sltev[-1])
                        ins = tensor.transpose(trpf[:, t % 2, 0:8],
                                               cmT[0:8, t * 128:(t + 1) * 128],
                                               idf[0:8, 0:8])
                        eT.inc(ins)

                    pev = eT.n

                    @blk.vector
                    def _(vector):
                        vector.wait_ge(eT.h, pev)
                        eV.inc(vector.tensor_tensor(flatm[:, :], trpf[:, t % 2, 0:8],
                                                    eoff[:, :], ALU.add))
                        eV.inc(vector.tensor_scalar(ovf[:, :], trpf[:, t % 2, 0:8],
                                                    float(CAP) + 0.5, None, ALU.is_ge))
                        vector.wait_ge(eV.h, eV.n)
                        eV.inc(vector.scalar_tensor_tensor(flatm[:, :], ovf[:, :],
                                                           1e9, flatm[:, :],
                                                           ALU.mult, ALU.add))
                        eV.inc(vector.tensor_scalar(flatm[:, :], flatm[:, :], -1.0,
                                                    None, ALU.add))
                        vector.wait_ge(eV.h, eV.n)
                        eV.inc(vector.tensor_tensor(m2tk[:, :], m1b[:, t, :],
                                                    flatm[:, :], ALU.mult))
                        vector.wait_ge(eV.h, eV.n)
                        eV.inc(vector.reduce_sum(sltf[:, 0, t:t + 1], m2tk[:, :],
                                                 axis=AX.X))
                        eV.inc(vector.tensor_sub(m2tk[:, :], cmb[:, t, :], m1b[:, t, :]))
                        vector.wait_ge(eV.h, eV.n)
                        eV.inc(vector.tensor_mul(m2tk[:, :], m2tk[:, :], flatm[:, :]))
                        vector.wait_ge(eV.h, eV.n)
                        ins = vector.reduce_sum(sltf[:, 1, t:t + 1], m2tk[:, :],
                                                axis=AX.X)
                        eV.inc(ins)

                    sltev.append(eV.n)

                @blk.vector
                def _(vector):
                    vector.wait_ge(eV.h, sltev[-1])
                    eV.inc(vector.tensor_copy(slot1i[:, :], sltf[:, 0, :]))
                    eV.inc(vector.tensor_copy(slot2i[:, :], sltf[:, 1, :]))

                slotev = eV.n

                scatev = []
                xn2lev = []
                for t in range(NOWN):
                    @blk.sync
                    def _(sync):
                        if t >= 2:
                            sync.wait_ge(qG.h, scatev[t - 2])
                        xn2lev.append(qS.dinc(sync.dma_start(
                            out=xn2tl[:, t % 2, :],
                            in_=xn2scr[t * 128:(t + 1) * 128, :])))

                    @blk.gpsimd
                    def _(gpsimd):
                        gpsimd.wait_ge(qS.h, xn2lev[t])
                        if t == 0:
                            gpsimd.wait_ge(eV.h, slotev)
                        qG.dinc(gpsimd.indirect_dma_start(
                            out=xescr[:, :],
                            out_offset=bass.IndirectOffsetOnAxis(slot1i[:, t:t + 1], 0),
                            in_=xn2tl[:, t % 2, :], in_offset=None))
                        qG.dinc(gpsimd.indirect_dma_start(
                            out=xescr[:, :],
                            out_offset=bass.IndirectOffsetOnAxis(slot2i[:, t:t + 1], 0),
                            in_=xn2tl[:, t % 2, :], in_offset=None))

                    scatev.append(qG.n)

                gate_done = (eT.n, eV.n, eS.n, qG.n, qA.n, qS.n)

            NCT = CAP // 128
            if p4lvl < 2:
                return
            with ExitStack() as es:
                xeT = es.enter_context(nc.sbuf_tensor("xeT", [128, 2, NMC, CAP], BF16))
                abuf = es.enter_context(nc.sbuf_tensor("abuf", [128, NFT, CAP], BF16))
                wd_sb = es.enter_context(nc.sbuf_tensor("wd_sb", [128, NFT, 2048], BF16))
                wg_sb = es.enter_context(nc.sbuf_tensor("wg_sb", [128, 2, NMC, 128], BF16))
                wu_sb = es.enter_context(nc.sbuf_tensor("wu_sb", [128, 2, NMC, 128], BF16))
                wgf32 = es.enter_context(nc.sbuf_tensor("wgf32", [128, 2, NMC, 128], F32))
                wuf32 = es.enter_context(nc.sbuf_tensor("wuf32", [128, 2, NMC, 128], F32))
                sgbuf = es.enter_context(nc.sbuf_tensor("sgbuf", [128, 2, CAP], F32))
                ybuf = es.enter_context(nc.sbuf_tensor("ybuf", [128, 2, 2048], BF16))
                psA = es.enter_context(nc.psum_tensor("psA", [128, 2048], F32))
                psB = es.enter_context(nc.psum_tensor("psB", [128, 2048], F32))

                xelev = {}
                dn_mm, yevac, ystev, amulev, siluev, gmmev, ummev, wdev_ = \
                    {}, {}, {}, {}, {}, {}, {}, {}
                wgld, wuld, wgc, wuc = {}, {}, {}, {}
                NK = E * NFT
                seqk = [(e2, f2) for e2 in range(E) for f2 in range(NFT)]

                def emit_wg_load(k):
                    if k >= NK:
                        return
                    e2, f2 = seqk[k]

                    @blk.sync
                    def _(sync):
                        if k >= 2:
                            sync.wait_ge(eV.h, wgc[k - 2])
                        else:
                            sync.wait_ge(qG.h, gate_done[3])
                        wgld[k] = qS.dinc(sync.dma_start(
                            out=wgf32[:, k % 2, :, :],
                            in_=wgd[e2, :, f2 * 128:(f2 + 1) * 128].rearrange(
                                "(mc p) f -> p mc f", p=128)))

                def emit_wu_load(k):
                    if k >= NK:
                        return
                    e2, f2 = seqk[k]

                    @blk.scalar
                    def _(scalar):
                        if k >= 2:
                            scalar.wait_ge(eS.h, wuc[k - 2])
                        else:
                            scalar.wait_ge(qG.h, gate_done[3])
                        wuld[k] = qA.dinc(scalar.dma_start(
                            out=wuf32[:, k % 2, :, :],
                            in_=wud[e2, :, f2 * 128:(f2 + 1) * 128].rearrange(
                                "(mc p) f -> p mc f", p=128)))

                def emit_wg_cast(k):
                    if k >= NK or k in wgc:
                        return

                    @blk.vector
                    def _(vector):
                        vector.wait_ge(qS.h, wgld[k])
                        if k >= 2:
                            vector.wait_ge(eT.h, ummev[seqk[k - 2]])
                        else:
                            vector.wait_ge(qG.h, gate_done[3])
                        wgc[k] = eV.inc(vector.tensor_copy(
                            wg_sb[:, k % 2, :, :], wgf32[:, k % 2, :, :]))

                def emit_wu_cast(k):
                    if k >= NK or k in wuc:
                        return

                    @blk.scalar
                    def _(scalar):
                        scalar.wait_ge(qA.h, wuld[k])
                        if k >= 2:
                            scalar.wait_ge(eT.h, ummev[seqk[k - 2]])
                        else:
                            scalar.wait_ge(qG.h, gate_done[3])
                        wuc[k] = eS.inc(scalar.activation(
                            wu_sb[:, k % 2, :, :], wuf32[:, k % 2, :, :], AF.Identity))

                def emit_xeT(e2):
                    @blk.sync
                    def _(sync):
                        if e2 == 0:
                            sync.wait_ge(qG.h, gate_done[3])
                            sync.wait_ge(eT.h, gate_done[0])
                        elif e2 >= 2:
                            sync.wait_ge(eT.h, ummev[(e2 - 2, NFT - 1)])
                        for mc in range(NMC):
                            qA.dinc(sync.dma_start(
                                out=xeT[:, e2 % 2, mc, :],
                                in_=xescr[e2 * CAP:(e2 + 1) * CAP,
                                          mc * 128:(mc + 1) * 128],
                                transpose=True))

                    xelev[e2] = qA.n

                for k0 in range(2):
                    emit_wg_load(k0)
                    emit_wu_load(k0)
                emit_wg_cast(0)
                emit_wu_cast(0)
                emit_xeT(0)
                emit_xeT(1)

                for e in range(E):
                    for ft in range(NFT):
                        k = e * NFT + ft
                        pf = k % 2

                        if ft == 2:
                            @blk.gpsimd
                            def _(gpsimd):
                                if e >= 1:
                                    gpsimd.wait_ge(eT.h, dn_mm[(e - 1, NCT - 1)])
                                for fc in range(NFT):
                                    qG.dinc(gpsimd.dma_start(
                                        out=wd_sb[:, fc, :],
                                        in_=wdd[e, fc * 128:(fc + 1) * 128, :]))

                            wdev_[e] = qG.n

                        emit_wg_load(k + 2)

                        gups = psA if pf == 0 else psB

                        @blk.tensor
                        def _(tensor):
                            tensor.wait_ge(eV.h, wgc[k])
                            tensor.wait_ge(eS.h, wuc[k])
                            if ft == 0:
                                tensor.wait_ge(qA.h, xelev[e])
                            if k >= 2:
                                tensor.wait_ge(eS.h, siluev[seqk[k - 2]])
                                tensor.wait_ge(eV.h, amulev[seqk[k - 2]])
                            if ft == 0:
                                prevy = ystev.get((e - 1, NCT - 1)) if e >= 1 else None
                                if prevy:
                                    tensor.wait_ge(qG.h, prevy)
                            CSL = [(i, min(i + 512, CAP)) for i in range(0, CAP, 512)]
                            for mc in range(NMC):
                                for lo, hi in CSL:
                                    ins = tensor.matmul(
                                        gups[:, lo:hi],
                                        wg_sb[:, pf, mc, :],
                                        xeT[:, e % 2, mc, lo:hi],
                                        start=(mc == 0), stop=(mc == NMC - 1))
                            eT.inc(ins)
                            gmmev[(e, ft)] = eT.n
                            for mc in range(NMC):
                                for lo, hi in CSL:
                                    ins = tensor.matmul(
                                        gups[:, 1024 + lo:1024 + hi],
                                        wu_sb[:, pf, mc, :],
                                        xeT[:, e % 2, mc, lo:hi],
                                        start=(mc == 0), stop=(mc == NMC - 1))
                            eT.inc(ins)
                            ummev[(e, ft)] = eT.n

                        emit_wu_cast(k + 1)
                        emit_wu_load(k + 2)

                        @blk.scalar
                        def _(scalar):
                            scalar.wait_ge(eT.h, gmmev[(e, ft)])
                            if k >= 2:
                                scalar.wait_ge(eV.h, amulev[seqk[k - 2]])
                            eS.inc(scalar.activation(sgbuf[:, pf, :], gups[:, 0:CAP],
                                                     AF.Silu))

                        siluev[(e, ft)] = eS.n

                        emit_wg_cast(k + 1)

                        @blk.vector
                        def _(vector):
                            vector.wait_ge(eS.h, siluev[(e, ft)])
                            vector.wait_ge(eT.h, ummev[(e, ft)])
                            if e >= 1 and ft == 0:
                                vector.wait_ge(eT.h, dn_mm[(e - 1, NCT - 1)])
                            ins = vector.tensor_tensor(abuf[:, ft, :],
                                                       gups[:, 1024:1024 + CAP],
                                                       sgbuf[:, pf, :], ALU.mult)
                            eV.inc(ins)

                        amulev[(e, ft)] = eV.n

                    if e + 2 < E:
                        emit_xeT(e + 2)
                    pfl = (e * NFT + NFT - 1) % 2
                    for u in range(NCT):
                        yph = psA if (pfl + u) % 2 == 0 else psB

                        @blk.tensor
                        def _(tensor):
                            if u == 0:
                                tensor.wait_ge(qG.h, wdev_[e])
                                tensor.wait_ge(eV.h, amulev[(e, NFT - 1)])
                                tensor.wait_ge(eS.h, siluev[(e, NFT - 1)])
                            if u == 1:
                                tensor.wait_ge(eV.h, amulev[(e, NFT - 2)])
                                tensor.wait_ge(eS.h, siluev[(e, NFT - 2)])
                            if u == 1 and e >= 1:
                                prevy = ystev.get((e - 1, NCT - 1))
                                if prevy:
                                    tensor.wait_ge(qG.h, prevy)
                            prevy = ystev.get((e, u - 2)) if u >= 2 else None
                            if prevy:
                                tensor.wait_ge(qG.h, prevy)
                            for fc in range(NFT):
                                for q4 in range(4):
                                    ins = tensor.matmul(
                                        yph[:, q4 * 512:(q4 + 1) * 512],
                                        abuf[:, fc, u * 128:(u + 1) * 128],
                                        wd_sb[:, fc, q4 * 512:(q4 + 1) * 512],
                                        start=(fc == 0), stop=(fc == NFT - 1))
                            eT.inc(ins)

                        dn_mm[(e, u)] = eT.n

                        @blk.scalar
                        def _(scalar):
                            scalar.wait_ge(eT.h, dn_mm[(e, u)])
                            prevst = ystev.get((e, u - 2)) if u >= 2 else \
                                (ystev.get((e - 1, NCT - 2 + u)) if e >= 1 else None)
                            if prevst:
                                scalar.wait_ge(qG.h, prevst)
                            eS.inc(scalar.activation(ybuf[:, u % 2, :], yph[:, :],
                                                     AF.Identity))

                        yevac[(e, u)] = eS.n

                        @blk.gpsimd
                        def _(gpsimd):
                            gpsimd.wait_ge(eS.h, yevac[(e, u)])
                            ystev[(e, u)] = qG.dinc(gpsimd.dma_start(
                                out=yescr[e * CAP + u * 128:e * CAP + (u + 1) * 128, :],
                                in_=ybuf[:, u % 2, :]))

                ye_done = (qG.n, eS.n)

            if p4lvl < 3:
                return
            with ExitStack() as es:
                DC = 4
                y1sb = es.enter_context(nc.sbuf_tensor("y1sb", [128, DC, M], BF16))
                y2sb = es.enter_context(nc.sbuf_tensor("y2sb", [128, DC, M], BF16))
                xmtl = es.enter_context(nc.sbuf_tensor("xmtl", [128, DC, M], F32))
                otl = es.enter_context(nc.sbuf_tensor("otl", [128, DC, M], F32))

                gthev = {}
                xmlev = {}
                cmbv = {}
                outev = {}

                for t in range(NOWN):
                    pt = t % DC

                    @blk.gpsimd
                    def _(gpsimd):
                        if t == 0:
                            gpsimd.wait_ge(qG.h, ye_done[0])
                        if t >= DC:
                            gpsimd.wait_ge(eV.h, cmbv[t - DC])
                        qG.dinc(gpsimd.indirect_dma_start(
                            out=y1sb[:, pt, :], out_offset=None, in_=yescr[:, :],
                            in_offset=bass.IndirectOffsetOnAxis(slot1i[:, t:t + 1], 0)))
                        qG.dinc(gpsimd.indirect_dma_start(
                            out=y2sb[:, pt, :], out_offset=None, in_=yescr[:, :],
                            in_offset=bass.IndirectOffsetOnAxis(slot2i[:, t:t + 1], 0)))

                    gthev[t] = qG.n

                    @blk.scalar
                    def _(scalar):
                        if t == 0:
                            scalar.wait_ge(eS.h, ye_done[1])
                        if t >= DC:
                            scalar.wait_ge(eV.h, cmbv[t - DC])
                        xmlev[t] = qA.dinc(scalar.dma_start(
                            out=xmtl[:, pt, :],
                            in_=xmidscr[t * 128:(t + 1) * 128, :]))

                    @blk.vector
                    def _(vector):
                        vector.wait_ge(qG.h, gthev[t])
                        vector.wait_ge(qA.h, xmlev[t])
                        if t >= DC:
                            vector.wait_ge(qS.h, outev[t - DC])
                        eV.inc(vector.scalar_tensor_tensor(
                            otl[:, pt, :], y1sb[:, pt, :], c1all[:, t:t + 1],
                            xmtl[:, pt, :], ALU.mult, ALU.add))
                        ins = vector.scalar_tensor_tensor(
                            otl[:, pt, :], y2sb[:, pt, :], c2all[:, t:t + 1],
                            otl[:, pt, :], ALU.mult, ALU.add)
                        eV.inc(ins)

                    cmbv[t] = eV.n

                    @blk.sync
                    def _(sync):
                        sync.wait_ge(eV.h, cmbv[t])
                        outev[t] = qS.dinc(sync.dma_start(
                            out=outd[t * 128:(t + 1) * 128, :], in_=otl[:, pt, :]))

        if phases >= 4:
            phase4()

        @blk.sync
        def _(sync):
            sync.wait_ge(qS.h, qS.n)
            sync.wait_ge(qA.h, qA.n)
            sync.wait_ge(qG.h, qG.n)

    return nc


def _prep_inputs(core, inputs):
    x = np.asarray(inputs["L0_attn_x"])
    b, c = core // 4, core % 4
    taus = [4 * s + c for s in range(16)]
    rows = np.concatenate([np.arange(t * 128, (t + 1) * 128) for t in taus])

    w_aq = np.asarray(inputs["L0_attn_w_aq"])
    w_ak = np.asarray(inputs["L0_attn_w_ak"])
    w_av = np.asarray(inputs["L0_attn_w_av"])
    w_ao = np.asarray(inputs["L0_attn_w_ao"])

    wq = np.ascontiguousarray(np.transpose(w_aq, (2, 0, 1, 3)).reshape(M, G * H * DH))
    wkv = np.ascontiguousarray(np.concatenate(
        [np.transpose(w_ak, (1, 0, 2)).reshape(M, G * DH),
         np.transpose(w_av, (1, 0, 2)).reshape(M, G * DH)], axis=1))
    wao = np.ascontiguousarray(w_ao.reshape(G * H, DH, M))

    freqs = np.float32(10000.0) ** (-np.arange(64, dtype=np.float32) / np.float32(64.0))
    posk = np.arange(T, dtype=np.float32)
    angk = posk[:, None] * freqs[None, :]
    ck = np.ascontiguousarray(np.tile(np.cos(angk).astype(np.float32), (1, G)))
    sk = np.ascontiguousarray(np.tile(np.sin(angk).astype(np.float32), (1, G)))

    posq = rows.astype(np.float32)
    angq = posq[:, None] * freqs[None, :]
    qk2 = np.float32(np.float32(DH ** -0.25) * np.float32(DH ** -0.25))
    cq = np.ascontiguousarray(np.tile((np.cos(angq) * qk2).astype(np.float32), (1, G * H)))
    sq = np.ascontiguousarray(np.tile((np.sin(angq) * qk2).astype(np.float32), (1, G * H)))

    mk = np.zeros((NSP, 8, 128, 256), dtype=np.float32)
    jj = np.arange(128)
    for sp in range(NSP):
        for mcl in range(8):
            mc = 8 * sp + mcl
            jglob = mc * 128 + jj
            for tt in range(2):
                tau = 4 * (2 * sp + tt) + c
                iglob = tau * 128 + np.arange(128)
                mk[sp, mcl, :, tt * 128:(tt + 1) * 128] = \
                    (jglob[:, None] <= iglob[None, :])
    masks = mk.astype(ml_dtypes.bfloat16)

    return {
        "xb": np.ascontiguousarray(x[b]),
        "xown": np.ascontiguousarray(x[b][rows]),
        "wq": wq, "wkv": wkv, "wao": wao,
        "gw": np.ascontiguousarray(np.asarray(inputs["L0_ffn_gate_w"])),
        "wgt": np.ascontiguousarray(np.asarray(inputs["L0_ffn_w_gate"])),
        "wup": np.ascontiguousarray(np.asarray(inputs["L0_ffn_w_up"])),
        "wdn": np.ascontiguousarray(np.asarray(inputs["L0_ffn_w_down"])),
        "ck": ck, "sk": sk, "cq": cq, "sq": sq,
        "masks": masks,
        "ones_in": np.ones((128, 128), dtype=ml_dtypes.bfloat16),
        "idf32": np.eye(128, dtype=np.float32),
        "eoffr": np.tile((np.arange(8, dtype=np.float32) * np.float32(CAP))[None, :],
                         (128, 1)),
    }, rows, b


def kernel(**inputs):
    nc = build_program()
    in_maps = []
    metas = []
    for core in range(8):
        im, rows, b = _prep_inputs(core, inputs)
        in_maps.append(im)
        metas.append((rows, b))
    res = run_bass_kernel_spmd(nc, in_maps, core_ids=list(range(8)))
    out = np.empty((B, T, M), dtype=np.float32)
    for core in range(8):
        rows, b = metas[core]
        out[b][rows] = res.results[core]["out"]
    return out


# version 22
# speedup vs baseline: 1.0618x; 1.0618x over previous
import numpy as np
import ml_dtypes
from contextlib import ExitStack

import concourse.bass as bass
import concourse.mybir as mybir
from concourse.bass_utils import run_bass_kernel_spmd

F32 = mybir.dt.float32
BF16 = mybir.dt.bfloat16
I32 = mybir.dt.int32
AF = mybir.ActivationFunctionType
ALU = mybir.AluOpType
AX = mybir.AxisListType

B, T, M = 2, 8192, 2048
G, H, DH = 4, 4, 128
FF, E = 1408, 8
TPC = 2048
NOWN = 16
NSP = 8
NFT = 11
NMC = 16
EPS = 1e-5
EXP_BIAS = -20.0
CAP = 768


class SemC:
    __slots__ = ("h", "n")

    def __init__(self, h):
        self.h = h
        self.n = 0

    def inc(self, instr, k=1):
        instr.then_inc(self.h, k)
        self.n += k
        return self.n

    def dinc(self, instr):
        return self.inc(instr, 16)


def build_program(debug=False, phases=4, p4lvl=5):
    nc = bass.Bass("TRN2", target_bir_lowering=False)
    kind_scr = "ExternalOutput" if debug else "Internal"

    xbd = nc.dram_tensor("xb", [T, M], F32, kind="ExternalInput").ap()
    xownd = nc.dram_tensor("xown", [TPC, M], F32, kind="ExternalInput").ap()
    wqd = nc.dram_tensor("wq", [M, G * H * DH], F32, kind="ExternalInput").ap()
    wkvd = nc.dram_tensor("wkv", [M, 2 * G * DH], F32, kind="ExternalInput").ap()
    waod = nc.dram_tensor("wao", [G * H, DH, M], F32, kind="ExternalInput").ap()
    gwd = nc.dram_tensor("gw", [M, E], F32, kind="ExternalInput").ap()
    wgd = nc.dram_tensor("wgt", [E, M, FF], F32, kind="ExternalInput").ap()
    wud = nc.dram_tensor("wup", [E, M, FF], F32, kind="ExternalInput").ap()
    wdd = nc.dram_tensor("wdn", [E, FF, M], F32, kind="ExternalInput").ap()
    ckd = nc.dram_tensor("ck", [T, G * 64], F32, kind="ExternalInput").ap()
    skd = nc.dram_tensor("sk", [T, G * 64], F32, kind="ExternalInput").ap()
    cqd = nc.dram_tensor("cq", [TPC, G * H * 64], F32, kind="ExternalInput").ap()
    sqd = nc.dram_tensor("sq", [TPC, G * H * 64], F32, kind="ExternalInput").ap()
    maskd = nc.dram_tensor("masks", [NSP, 8, 128, 256], BF16, kind="ExternalInput").ap()
    idf32d = nc.dram_tensor("idf32", [128, 128], F32, kind="ExternalInput").ap()
    eoffd = nc.dram_tensor("eoffr", [128, 8], F32, kind="ExternalInput").ap()
    onesd = nc.dram_tensor("ones_in", [128, 128], BF16, kind="ExternalInput").ap()
    outd = nc.dram_tensor("out", [TPC, M], F32, kind="ExternalOutput").ap()

    xnscr = nc.dram_tensor("xnscr", [T, M], BF16, kind=kind_scr).ap()
    xnoscr = nc.dram_tensor("xnoscr", [TPC, M], BF16, kind=kind_scr).ap()
    kvscr = nc.dram_tensor("kvscr", [T, 2 * G * DH], BF16, kind=kind_scr).ap()
    qscr = nc.dram_tensor("qscr", [TPC, G * H * DH], BF16, kind=kind_scr).ap()
    xmidscr = nc.dram_tensor("xmidscr", [TPC, M], F32, kind=kind_scr).ap()
    xn2scr = nc.dram_tensor("xn2scr", [TPC, M], BF16, kind=kind_scr).ap()
    xescr = nc.dram_tensor("xescr", [E * CAP, M], BF16, kind=kind_scr).ap()
    yescr = nc.dram_tensor("yescr", [E * CAP, M], BF16, kind=kind_scr).ap()

    ctx = ExitStack()
    with ctx:
        qS = SemC(ctx.enter_context(nc.semaphore("qS")))
        qA = SemC(ctx.enter_context(nc.semaphore("qA")))
        qG = SemC(ctx.enter_context(nc.semaphore("qG")))
        eT = SemC(ctx.enter_context(nc.semaphore("eT")))
        eV = SemC(ctx.enter_context(nc.semaphore("eV")))
        eS = SemC(ctx.enter_context(nc.semaphore("eS")))
        ALLSEM = (qS, qA, qG, eT, eV, eS)

        ones_sb = ctx.enter_context(nc.sbuf_tensor("ones_sb", [128, 128], BF16))
        cbias = ctx.enter_context(nc.sbuf_tensor("cbias", [128, 1], F32))
        ceps = ctx.enter_context(nc.sbuf_tensor("ceps", [128, 1], F32))
        blk = ctx.enter_context(nc.Block())

        def barrier():
            snap = [(s, s.n) for s in ALLSEM]

            def mk(eng):
                for s, n in snap:
                    if n:
                        eng.wait_ge(s.h, n)
            blk.sync(mk)
            blk.scalar(mk)
            blk.vector(mk)
            blk.tensor(mk)
            blk.gpsimd(mk)

        @blk.sync
        def _(sync):
            qS.dinc(sync.dma_start(out=ones_sb[:, :], in_=onesd[:, :]))

        @blk.vector
        def _(vector):
            eV.inc(vector.memset(cbias[:, :], EXP_BIAS))
            eV.inc(vector.memset(ceps[:, :], EPS))

        def emit_ln_tile(src_ap, dst_scr, tau, stbufs, ev):
            xbuf, xnb, bns, bna, rstd, nbuf = stbufs
            i = ev["i"]
            ev["i"] += 1
            p = i % 2
            r0 = tau * 128
            prev_apply = ev["apply"][i - 2] if i >= 2 else 0
            prev_store = ev["store"][i - 2] if i >= 2 else 0

            @blk.sync
            def _(sync):
                if prev_apply:
                    sync.wait_ge(eS.h, prev_apply)
                ev["load"].append(qS.dinc(
                    sync.dma_start(out=xbuf[:, p, :], in_=src_ap[r0:r0 + 128, :])))

            myload = ev["load"][i]

            @blk.vector
            def _(vector):
                vector.wait_ge(qS.h, myload)
                if prev_apply:
                    vector.wait_ge(eS.h, prev_apply)
                for c in range(4):
                    ins = vector.bn_stats(bns[:, p, c, :], xbuf[:, p, c * 512:(c + 1) * 512])
                eV.inc(ins)
                vector.wait_ge(eV.h, eV.n)
                eV.inc(vector.bn_aggr(bna[:, p, :], bns[:, p, :, :]))
                vector.wait_ge(eV.h, eV.n)
                eV.inc(vector.tensor_scalar(rstd[:, p, :], bna[:, p, 1:2], 1.0, EPS,
                                            ALU.mult, ALU.add))
                vector.wait_ge(eV.h, eV.n)
                eV.inc(vector.reciprocal(rstd[:, p, :], rstd[:, p, :]))

            v1 = eV.n

            @blk.scalar
            def _(scalar):
                scalar.wait_ge(eV.h, v1)
                eS.inc(scalar.activation(rstd[:, p, :], rstd[:, p, :], AF.Sqrt))

            s1 = eS.n

            @blk.vector
            def _(vector):
                vector.wait_ge(eS.h, s1)
                eV.inc(vector.tensor_scalar(nbuf[:, p, :], bna[:, p, 0:1], rstd[:, p, 0:1],
                                            -1.0, ALU.mult, ALU.mult))

            v2 = eV.n

            @blk.scalar
            def _(scalar):
                scalar.wait_ge(eV.h, v2)
                if prev_store:
                    scalar.wait_ge(qS.h, prev_store)
                ev["apply"].append(eS.inc(
                    scalar.activation(xnb[:, p, :], xbuf[:, p, :], AF.Identity,
                                      bias=nbuf[:, p, :], scale=rstd[:, p, :])))

            myapply = ev["apply"][i]

            @blk.sync
            def _(sync):
                sync.wait_ge(eS.h, myapply)
                ev["store"].append(qS.dinc(
                    sync.dma_start(out=dst_scr[r0:r0 + 128, :], in_=xnb[:, p, :])))

        def phase1():
          with ExitStack() as es:
            xbuf = es.enter_context(nc.sbuf_tensor("xbuf", [128, 4, M], F32))
            xnb = es.enter_context(nc.sbuf_tensor("xnb", [128, 2, M], BF16))
            bns = es.enter_context(nc.sbuf_tensor("bns", [128, 4, 4, 6], F32))
            bna = es.enter_context(nc.sbuf_tensor("bna", [128, 4, 2], F32))
            lnt = es.enter_context(nc.sbuf_tensor("lnt", [128, 4, 1], F32))
            rsall = es.enter_context(nc.sbuf_tensor("rsall", [128, 80], F32))
            xnT = es.enter_context(nc.sbuf_tensor("xnT", [128, 2, NMC, 1024], BF16))

            jobs = [(xownd, xnoscr, t) for t in range(NOWN)] + \
                   [(xbd, xnscr, t) for t in range(64)]
            NJ = len(jobs)
            loadv = [0] * NJ
            bnev = [0] * NJ
            agev = [0] * NJ
            rsallv = [0] * NJ
            applyv = [0] * NJ
            storev = [0] * NJ

            def ln_load(i):
                src_ap, _, tau = jobs[i]
                p = i % 4

                @blk.sync
                def _(sync):
                    if i >= 4:
                        sync.wait_ge(eV.h, applyv[i - 4])
                    loadv[i] = qS.dinc(sync.dma_start(
                        out=xbuf[:, p, :], in_=src_ap[tau * 128:tau * 128 + 128, :]))

            def ln_A(i):
                p = i % 4

                @blk.vector
                def _(vector):
                    vector.wait_ge(qS.h, loadv[i])
                    if i >= 4:
                        vector.wait_ge(eV.h, applyv[i - 4])
                    for c in range(4):
                        ins = vector.bn_stats(bns[:, p, c, :],
                                              xbuf[:, p, c * 512:(c + 1) * 512])
                    bnev[i] = eV.inc(ins)

            def ln_B(i):
                p = i % 4

                @blk.vector
                def _(vector):
                    vector.wait_ge(eV.h, bnev[i])
                    agev[i] = eV.inc(vector.bn_aggr(bna[:, p, :], bns[:, p, :, :]))

            def ln_S(i):
                p = i % 4

                @blk.scalar
                def _(scalar):
                    scalar.wait_ge(eV.h, agev[i])
                    eS.inc(scalar.activation(lnt[:, p, :], bna[:, p, 1:2], AF.Ln,
                                             bias=ceps[:, 0:1]))
                    scalar.wait_ge(eS.h, eS.n)
                    rsallv[i] = eS.inc(scalar.activation(rsall[:, i:i + 1],
                                                         lnt[:, p, :], AF.Exp,
                                                         scale=-0.5))

            def ln_C(i):
                _, dst_scr, tau = jobs[i]
                p = i % 4

                @blk.vector
                def _(vector):
                    vector.wait_ge(eV.h, agev[i])
                    if i >= 2:
                        vector.wait_ge(qA.h, storev[i - 2])
                    applyv[i] = eV.inc(vector.tensor_scalar(
                        xnb[:, i % 2, :], xbuf[:, p, :], bna[:, p, 0:1], None,
                        ALU.subtract))

                @blk.scalar
                def _(scalar):
                    scalar.wait_ge(eV.h, applyv[i])
                    storev[i] = qA.dinc(scalar.dma_start(
                        out=dst_scr[tau * 128:tau * 128 + 128, :], in_=xnb[:, i % 2, :]))

            def ln_run(lo, hi):
                for i in range(lo, hi + 1):
                    if i < hi:
                        ln_load(i)
                        ln_A(i)
                    if lo <= i - 1 < hi:
                        ln_B(i - 1)
                        ln_S(i - 1)
                        ln_C(i - 1)

            ln_run(0, 8)

            qmm, qropev, qstev, cqev, trl2ev = [], [], [], [], []

            with ExitStack() as esb:
                wq_sb = esb.enter_context(nc.sbuf_tensor("wq_sb", [128, NMC, 2048], BF16))
                cqb = esb.enter_context(nc.sbuf_tensor("cqb", [128, 2, G * H, 64], F32))
                sqb = esb.enter_context(nc.sbuf_tensor("sqb", [128, 2, G * H, 64], F32))
                qtmp = esb.enter_context(nc.sbuf_tensor("qtmp", [128, 2, G * H, 64], F32))
                qbuf = esb.enter_context(nc.sbuf_tensor("qbuf", [128, 2, 2048], BF16))
                qps = esb.enter_context(nc.psum_tensor("qps", [128, 2, 2048], F32))

                @blk.gpsimd
                def _(gpsimd):
                    qG.dinc(gpsimd.dma_start(
                        out=wq_sb[:, :, :],
                        in_=wqd.rearrange("(mc p) n -> p mc n", p=128)))

                wq_ev = qG.n

                for OS in range(2):
                    store_upto = storev[8 * OS + 7]

                    @blk.scalar
                    def _(scalar):
                        scalar.wait_ge(qA.h, store_upto)
                        for mc in range(NMC):
                            qA.dinc(scalar.dma_start(
                                out=xnT[:, OS % 2, mc, :],
                                in_=xnoscr[8 * OS * 128:(8 * OS + 8) * 128,
                                           mc * 128:(mc + 1) * 128],
                                transpose=True))

                    trl2ev.append(qA.n)
                    if OS == 0:
                        ln_run(8, NOWN)

                    for t8 in range(8):
                        ti = 8 * OS + t8
                        pq = ti % 2

                        @blk.tensor
                        def _(tensor):
                            tensor.wait_ge(qA.h, trl2ev[OS])
                            if ti == 0:
                                tensor.wait_ge(qG.h, wq_ev)
                            if ti >= 2:
                                tensor.wait_ge(eV.h, qropev[ti - 2])
                            for mc in range(NMC):
                                for q4 in range(4):
                                    ins = tensor.matmul(
                                        qps[:, pq, q4 * 512:(q4 + 1) * 512],
                                        xnT[:, OS % 2, mc, t8 * 128:(t8 + 1) * 128],
                                        wq_sb[:, mc, q4 * 512:(q4 + 1) * 512],
                                        start=(mc == 0), stop=(mc == NMC - 1))
                            eT.inc(ins)

                        qmm.append(eT.n)

                        @blk.sync
                        def _(sync):
                            if ti >= 2:
                                sync.wait_ge(eV.h, qropev[ti - 2])
                            qS.dinc(sync.dma_start(
                                out=cqb[:, pq, :, :],
                                in_=cqd[ti * 128:(ti + 1) * 128, :].rearrange(
                                    "p (h f) -> p h f", h=G * H)))
                            qS.dinc(sync.dma_start(
                                out=sqb[:, pq, :, :],
                                in_=sqd[ti * 128:(ti + 1) * 128, :].rearrange(
                                    "p (h f) -> p h f", h=G * H)))

                        cqev.append(qS.n)

                        @blk.vector
                        def _(vector):
                            vector.wait_ge(eT.h, qmm[ti])
                            vector.wait_ge(qS.h, cqev[ti])
                            vector.wait_ge(eS.h, rsallv[ti])
                            if ti >= 2:
                                vector.wait_ge(qG.h, qstev[ti - 2])
                            rs = rsall[:, ti:ti + 1]
                            qview = qps[:, pq, :].rearrange("p (h t f) -> p h t f",
                                                            h=G * H, t=2)
                            qout = qbuf[:, pq, :].rearrange("p (h t f) -> p h t f",
                                                            h=G * H, t=2)
                            qe, qo = qview[:, :, 0, :], qview[:, :, 1, :]
                            vector.scalar_tensor_tensor(qtmp[:, 0, :, :], qe, rs,
                                                        cqb[:, pq, :, :], ALU.mult, ALU.mult)
                            vector.scalar_tensor_tensor(qtmp[:, 1, :, :], qo, rs,
                                                        sqb[:, pq, :, :], ALU.mult, ALU.mult)
                            vector.tensor_tensor(qout[:, :, 0, :], qtmp[:, 0, :, :],
                                                 qtmp[:, 1, :, :], ALU.subtract)
                            vector.scalar_tensor_tensor(qtmp[:, 0, :, :], qe, rs,
                                                        sqb[:, pq, :, :], ALU.mult, ALU.mult)
                            vector.scalar_tensor_tensor(qtmp[:, 1, :, :], qo, rs,
                                                        cqb[:, pq, :, :], ALU.mult, ALU.mult)
                            ins = vector.tensor_tensor(qout[:, :, 1, :], qtmp[:, 0, :, :],
                                                       qtmp[:, 1, :, :], ALU.add)
                            eV.inc(ins)

                        qropev.append(eV.n)

                        @blk.gpsimd
                        def _(gpsimd):
                            gpsimd.wait_ge(eV.h, qropev[ti])
                            qstev.append(qG.dinc(gpsimd.dma_start(
                                out=qscr[ti * 128:(ti + 1) * 128, :],
                                in_=qbuf[:, pq, :])))

            q_done = (qmm[-1], qropev[-1], qstev[-1])
            with ExitStack() as esa:
                wkv_sb = esa.enter_context(nc.sbuf_tensor("wkv_sb", [128, NMC, 1024], BF16))
                ckb = esa.enter_context(nc.sbuf_tensor("ckb", [128, 2, G, 64], F32))
                skb = esa.enter_context(nc.sbuf_tensor("skb", [128, 2, G, 64], F32))
                rtmp = esa.enter_context(nc.sbuf_tensor("rtmp", [128, 2, G, 64], F32))
                kvbuf = esa.enter_context(nc.sbuf_tensor("kvbuf", [128, 2, 1024], BF16))
                kvps = esa.enter_context(nc.psum_tensor("kvps", [128, 2, 1024], F32))

                @blk.gpsimd
                def _(gpsimd):
                    gpsimd.wait_ge(eT.h, q_done[0])
                    gpsimd.wait_ge(eV.h, q_done[1])
                    gpsimd.wait_ge(qG.h, q_done[2])
                    qG.dinc(gpsimd.dma_start(
                        out=wkv_sb[:, :, :],
                        in_=wkvd.rearrange("(mc p) n -> p mc n", p=128)))

                wkv_ev = qG.n
                kvmm, ropev, vcpev, kvstev, ckskev, trlev = [], [], [], [], [], []

                ln_run(NOWN, NOWN + 8)
                for S in range(8):
                    store_upto = storev[NOWN + 8 * S + 7]

                    @blk.scalar
                    def _(scalar):
                        scalar.wait_ge(qA.h, store_upto)
                        if S >= 2:
                            scalar.wait_ge(eT.h, kvmm[8 * (S - 2) + 7])
                        else:
                            scalar.wait_ge(eT.h, q_done[0])
                        for mc in range(NMC):
                            qA.dinc(scalar.dma_start(
                                out=xnT[:, S % 2, mc, :],
                                in_=xnscr[8 * S * 128:(8 * S + 8) * 128,
                                          mc * 128:(mc + 1) * 128],
                                transpose=True))

                    trlev.append(qA.n)
                    if S + 1 < 8:
                        ln_run(NOWN + 8 * (S + 1), NOWN + 8 * (S + 1) + 8)

                    for t8 in range(8):
                        tau = 8 * S + t8
                        pk = tau % 2

                        @blk.tensor
                        def _(tensor):
                            tensor.wait_ge(qA.h, trlev[S])
                            if tau == 0:
                                tensor.wait_ge(qG.h, wkv_ev)
                            if tau >= 2:
                                tensor.wait_ge(eV.h, ropev[tau - 2])
                            for mc in range(NMC):
                                tensor.matmul(kvps[:, pk, 0:512],
                                              xnT[:, S % 2, mc, t8 * 128:(t8 + 1) * 128],
                                              wkv_sb[:, mc, 0:512],
                                              start=(mc == 0), stop=(mc == NMC - 1))
                                ins = tensor.matmul(kvps[:, pk, 512:1024],
                                                    xnT[:, S % 2, mc, t8 * 128:(t8 + 1) * 128],
                                                    wkv_sb[:, mc, 512:1024],
                                                    start=(mc == 0), stop=(mc == NMC - 1))
                            eT.inc(ins)

                        kvmm.append(eT.n)

                        @blk.sync
                        def _(sync):
                            if tau >= 2:
                                sync.wait_ge(eV.h, ropev[tau - 2])
                            qS.dinc(sync.dma_start(
                                out=ckb[:, pk, :, :],
                                in_=ckd[tau * 128:(tau + 1) * 128, :].rearrange(
                                    "p (g f) -> p g f", g=G)))
                            qS.dinc(sync.dma_start(
                                out=skb[:, pk, :, :],
                                in_=skd[tau * 128:(tau + 1) * 128, :].rearrange(
                                    "p (g f) -> p g f", g=G)))

                        ckskev.append(qS.n)

                        @blk.vector
                        def _(vector):
                            vector.wait_ge(eT.h, kvmm[tau])
                            vector.wait_ge(qS.h, ckskev[tau])
                            vector.wait_ge(eS.h, rsallv[NOWN + tau])
                            if tau >= 2:
                                vector.wait_ge(qG.h, kvstev[tau - 2])
                            rs = rsall[:, NOWN + tau:NOWN + tau + 1]
                            kview = kvps[:, pk, 0:512].rearrange("p (g t f) -> p g t f",
                                                                 g=G, t=2)
                            kout = kvbuf[:, pk, 0:512].rearrange("p (g t f) -> p g t f",
                                                                 g=G, t=2)
                            ke, ko = kview[:, :, 0, :], kview[:, :, 1, :]
                            vector.scalar_tensor_tensor(rtmp[:, 0, :, :], ke, rs,
                                                        ckb[:, pk, :, :], ALU.mult, ALU.mult)
                            vector.scalar_tensor_tensor(rtmp[:, 1, :, :], ko, rs,
                                                        skb[:, pk, :, :], ALU.mult, ALU.mult)
                            vector.tensor_tensor(kout[:, :, 0, :], rtmp[:, 0, :, :],
                                                 rtmp[:, 1, :, :], ALU.subtract)
                            vector.scalar_tensor_tensor(rtmp[:, 0, :, :], ke, rs,
                                                        skb[:, pk, :, :], ALU.mult, ALU.mult)
                            vector.scalar_tensor_tensor(rtmp[:, 1, :, :], ko, rs,
                                                        ckb[:, pk, :, :], ALU.mult, ALU.mult)
                            vector.tensor_tensor(kout[:, :, 1, :], rtmp[:, 0, :, :],
                                                 rtmp[:, 1, :, :], ALU.add)
                            ins = vector.tensor_scalar(kvbuf[:, pk, 512:1024],
                                                       kvps[:, pk, 512:1024], rs, None,
                                                       ALU.mult)
                            eV.inc(ins)

                        ropev.append(eV.n)

                        @blk.gpsimd
                        def _(gpsimd):
                            gpsimd.wait_ge(eV.h, ropev[tau])
                            kvstev.append(qG.dinc(gpsimd.dma_start(
                                out=kvscr[tau * 128:(tau + 1) * 128, :],
                                in_=kvbuf[:, pk, :])))

        if phases >= 1:
            phase1()

        barrier()
        oT_ctx = nc.sbuf_tensor("oT", [128, G * H, 2048], BF16)
        oT = oT_ctx.__enter__()

        def phase2():
          with ExitStack() as es:
            kT = es.enter_context(nc.sbuf_tensor("kT", [128, 2, T], BF16))
            vG = es.enter_context(nc.sbuf_tensor("vG", [128, 2, 64, 128], BF16))
            qT = es.enter_context(nc.sbuf_tensor("qT", [128, 2, H, 2048], BF16))
            maskb = es.enter_context(nc.sbuf_tensor("maskb", [128, 2, 8, 256], BF16))
            ptile = es.enter_context(nc.sbuf_tensor("ptile", [128, 2, H, 256], BF16))
            zln = es.enter_context(nc.sbuf_tensor("zln", [128, 1024], F32))
            zrec = es.enter_context(nc.sbuf_tensor("zrec", [128, 1024], F32))
            sps = es.enter_context(nc.psum_tensor("sps", [128, 2, 1024], F32))
            oacc = es.enter_context(nc.psum_tensor("oacc", [128, 1024], F32))
            zacc = es.enter_context(nc.psum_tensor("zacc", [128, 1024], F32))
            if True:

                gldev = []
                gmmdone = [0] * G
                onormev = {}
                zrecev = {}
                msk_last = {}

                def emit_group_loads(g):
                    gp = g % 2

                    @blk.sync
                    def _(sync):
                        if g >= 2:
                            sync.wait_ge(eT.h, gmmdone[g - 2])
                        qS.dinc(sync.dma_start(
                            out=kT[:, gp, :],
                            in_=kvscr[:, g * 128:(g + 1) * 128], transpose=True))
                        for h in range(H):
                            qS.dinc(sync.dma_start(
                                out=qT[:, gp, h, :],
                                in_=qscr[:, (g * H + h) * 128:(g * H + h + 1) * 128],
                                transpose=True))
                        qS.dinc(sync.dma_start(
                            out=vG[:, gp, :, :],
                            in_=kvscr[:, 512 + g * 128:512 + (g + 1) * 128].rearrange(
                                "(cc p) d -> p cc d", p=128)))

                    gldev.append((qS.n, qS.n, qS.n))

                emit_group_loads(0)
                emit_group_loads(1)

                for g in range(G):
                    gp = g % 2
                    for sp in range(NSP):
                        L = 8 * sp + 8
                        qsl = slice(sp * 256, (sp + 1) * 256)
                        T0 = eT.n
                        S0 = eS.n
                        V0 = eV.n

                        def f_sdone(mc):
                            return T0 + (1 if mc == 0 else 2 * mc)

                        def f_ozdone(mc):
                            return T0 + (2 * L if mc == L - 1 else 2 * mc + 3)

                        def f_mskev(mc):
                            return V0 + (mc - 8 * sp) + 1

                        @blk.sync
                        def _(sync):
                            prevkey = (g, sp - 2) if sp >= 2 else (g - 1, sp + 6)
                            prev = msk_last.get(prevkey, 0)
                            if prev:
                                sync.wait_ge(eV.h, prev)
                            qS.dinc(sync.dma_start(
                                out=maskb[:, sp % 2, :, :],
                                in_=maskd[sp, :, :, :].rearrange("m p i -> p m i")))

                        mskld = qS.n
                        prevkey = (g, sp - 1) if sp > 0 else (g - 1, NSP - 1)

                        @blk.scalar
                        def _(scalar):
                            for mc in range(L):
                                scalar.wait_ge(eT.h, f_sdone(mc))
                                eS.inc(scalar.activation(
                                    ptile[:, mc % 2, :, :],
                                    sps[:, mc % 2, :].rearrange("p (h i) -> p h i", h=H),
                                    AF.Exp, bias=cbias[:, 0:1]))
                            scalar.wait_ge(eT.h, T0 + 2 * L)
                            if prevkey in onormev:
                                scalar.wait_ge(eV.h, onormev[prevkey])
                            eS.inc(scalar.activation(zln[:, :], zacc[:, :], AF.Ln))
                            scalar.wait_ge(eS.h, eS.n)
                            eS.inc(scalar.activation(zrec[:, :], zln[:, :], AF.Exp,
                                                     scale=-1.0))

                        zrecev[(g, sp)] = eS.n
                        assert eS.n == S0 + L + 2

                        @blk.tensor
                        def _(tensor):
                            def emit_s(mc):
                                if mc == 0:
                                    tensor.wait_ge(qS.h, gldev[g][0])
                                    if prevkey in onormev:
                                        tensor.wait_ge(eV.h, onormev[prevkey])
                                        tensor.wait_ge(eS.h, zrecev[prevkey])
                                if mc >= 2:
                                    tensor.wait_ge(eS.h, S0 + mc - 1)
                                tensor.matmul(sps[:, mc % 2, 0:512],
                                              kT[:, gp, mc * 128:(mc + 1) * 128],
                                              qT[:, gp, 0:2, qsl],
                                              start=True, stop=True)
                                ins = tensor.matmul(sps[:, mc % 2, 512:1024],
                                                    kT[:, gp, mc * 128:(mc + 1) * 128],
                                                    qT[:, gp, 2:4, qsl],
                                                    start=True, stop=True)
                                eT.inc(ins)
                                assert eT.n == f_sdone(mc)

                            def emit_oz(mc):
                                tensor.wait_ge(eS.h, S0 + mc + 1)
                                if mc >= 8 * sp:
                                    tensor.wait_ge(eV.h, f_mskev(mc))
                                    if mc == 8 * sp:
                                        tensor.wait_ge(qS.h, mskld)
                                st_ = (mc == 0)
                                fin = (mc == L - 1)
                                par = mc % 2
                                tensor.matmul(oacc[:, 0:512], vG[:, gp, mc, :],
                                              ptile[:, par, 0:2, :], start=st_, stop=fin)
                                tensor.matmul(oacc[:, 512:1024], vG[:, gp, mc, :],
                                              ptile[:, par, 2:4, :], start=st_, stop=fin)
                                tensor.matmul(zacc[:, 0:512], ones_sb[:, :],
                                              ptile[:, par, 0:2, :], start=st_, stop=fin)
                                ins = tensor.matmul(zacc[:, 512:1024], ones_sb[:, :],
                                                    ptile[:, par, 2:4, :], start=st_, stop=fin)
                                eT.inc(ins)
                                assert eT.n == f_ozdone(mc)

                            for mc in range(L):
                                emit_s(mc)
                                if mc >= 1:
                                    emit_oz(mc - 1)
                            emit_oz(L - 1)

                        @blk.vector
                        def _(vector):
                            for mc in range(8 * sp, L):
                                mcl = mc - 8 * sp
                                vector.wait_ge(eS.h, S0 + mc + 1)
                                if mc == 8 * sp:
                                    vector.wait_ge(qS.h, mskld)
                                mb = maskb[:, sp % 2, mcl, :].unsqueeze(1) \
                                    .broadcast_to((128, H, 256))
                                ins = vector.tensor_tensor(ptile[:, mc % 2, :, :],
                                                           ptile[:, mc % 2, :, :],
                                                           mb, ALU.mult)
                                eV.inc(ins)
                                assert eV.n == f_mskev(mc)
                            vector.wait_ge(eS.h, zrecev[(g, sp)])
                            vector.wait_ge(eT.h, T0 + 2 * L)
                            ins = vector.tensor_tensor(
                                oT[:, g * H:(g + 1) * H, qsl],
                                oacc[:, :].rearrange("p (h i) -> p h i", h=H),
                                zrec[:, :].rearrange("p (h i) -> p h i", h=H),
                                ALU.mult)
                            eV.inc(ins)

                        msk_last[(g, sp)] = V0 + 8
                        onormev[(g, sp)] = eV.n

                    gmmdone[g] = eT.n
                    if g + 2 < G:
                        emit_group_loads(g + 2)

        phase2()

        barrier()

        def phase3():
          with ExitStack() as es:
            wao_sb = es.enter_context(nc.sbuf_tensor("wao_sb", [128, G * H, 2048], BF16))
            xob = es.enter_context(nc.sbuf_tensor("xob", [128, 2, 2048], F32))
            xmb = es.enter_context(nc.sbuf_tensor("xmb", [128, 2, 2048], F32))
            xn2b = es.enter_context(nc.sbuf_tensor("xn2b", [128, 2, 2048], BF16))
            bns3 = es.enter_context(nc.sbuf_tensor("bns3", [128, 2, 4, 6], F32))
            bna3 = es.enter_context(nc.sbuf_tensor("bna3", [128, 2, 2], F32))
            rstd3 = es.enter_context(nc.sbuf_tensor("rstd3", [128, 2, 1], F32))
            nb3 = es.enter_context(nc.sbuf_tensor("nb3", [128, 2, 1], F32))
            rps = es.enter_context(nc.psum_tensor("rps", [128, 2, 2048], F32))
            if True:

                waoev = []
                for gh in range(G * H):
                    @blk.gpsimd
                    def _(gpsimd):
                        qG.dinc(gpsimd.dma_start(out=wao_sb[:, gh, :],
                                                 in_=waod[gh, :, :]))
                    waoev.append(qG.n)

                rmm, xmidv, xldev, xn2ev, p3stev = [], [], [], [], []

                for t in range(NOWN):
                    pr = t % 2

                    @blk.tensor
                    def _(tensor):
                        if t >= 2:
                            tensor.wait_ge(eV.h, xmidv[t - 2])
                        for gh in range(G * H):
                            if t == 0:
                                tensor.wait_ge(qG.h, waoev[gh])
                            for q4 in range(4):
                                ins = tensor.matmul(
                                    rps[:, pr, q4 * 512:(q4 + 1) * 512],
                                    oT[:, gh, t * 128:(t + 1) * 128],
                                    wao_sb[:, gh, q4 * 512:(q4 + 1) * 512],
                                    start=(gh == 0), stop=(gh == G * H - 1))
                        eT.inc(ins)

                    rmm.append(eT.n)

                    @blk.sync
                    def _(sync):
                        if t >= 2:
                            sync.wait_ge(eV.h, xmidv[t - 2])
                        qS.dinc(sync.dma_start(out=xob[:, pr, :],
                                               in_=xownd[t * 128:(t + 1) * 128, :]))

                    xldev.append(qS.n)

                    @blk.vector
                    def _(vector):
                        vector.wait_ge(eT.h, rmm[t])
                        vector.wait_ge(qS.h, xldev[t])
                        if t >= 2:
                            vector.wait_ge(qS.h, p3stev[t - 2])
                        eV.inc(vector.tensor_add(xmb[:, pr, :], rps[:, pr, :],
                                                 xob[:, pr, :]))
                        vector.wait_ge(eV.h, eV.n)
                        for c in range(4):
                            ins = vector.bn_stats(bns3[:, pr, c, :],
                                                  xmb[:, pr, c * 512:(c + 1) * 512])
                        eV.inc(ins)
                        vector.wait_ge(eV.h, eV.n)
                        eV.inc(vector.bn_aggr(bna3[:, pr, :], bns3[:, pr, :, :]))
                        vector.wait_ge(eV.h, eV.n)
                        eV.inc(vector.tensor_scalar(rstd3[:, pr, :], bna3[:, pr, 1:2],
                                                    1.0, EPS, ALU.mult, ALU.add))
                        vector.wait_ge(eV.h, eV.n)
                        eV.inc(vector.reciprocal(rstd3[:, pr, :], rstd3[:, pr, :]))

                    xmidv.append(eV.n)
                    vv = eV.n

                    @blk.scalar
                    def _(scalar):
                        scalar.wait_ge(eV.h, vv)
                        eS.inc(scalar.activation(rstd3[:, pr, :], rstd3[:, pr, :], AF.Sqrt))

                    ss = eS.n

                    @blk.vector
                    def _(vector):
                        vector.wait_ge(eS.h, ss)
                        eV.inc(vector.tensor_scalar(nb3[:, pr, :], bna3[:, pr, 0:1],
                                                    rstd3[:, pr, 0:1], -1.0,
                                                    ALU.mult, ALU.mult))

                    vv2 = eV.n

                    @blk.scalar
                    def _(scalar):
                        scalar.wait_ge(eV.h, vv2)
                        if t >= 2:
                            scalar.wait_ge(qS.h, p3stev[t - 2])
                        eS.inc(scalar.activation(xn2b[:, pr, :], xmb[:, pr, :],
                                                 AF.Identity, bias=nb3[:, pr, :],
                                                 scale=rstd3[:, pr, :]))

                    xn2ev.append(eS.n)

                    @blk.sync
                    def _(sync):
                        sync.wait_ge(eS.h, xn2ev[t])
                        qS.dinc(sync.dma_start(out=xmidscr[t * 128:(t + 1) * 128, :],
                                               in_=xmb[:, pr, :]))
                        qS.dinc(sync.dma_start(out=xn2scr[t * 128:(t + 1) * 128, :],
                                               in_=xn2b[:, pr, :]))

                    p3stev.append(qS.n)

        if phases >= 3:
            phase3()
        oT_ctx.__exit__(None, None, None)

        barrier()

        def phase4():
          slot_scope = ExitStack()
          with slot_scope:
            slot1i = slot_scope.enter_context(nc.sbuf_tensor("slot1i", [128, 16], I32))
            slot2i = slot_scope.enter_context(nc.sbuf_tensor("slot2i", [128, 16], I32))
            c1all = slot_scope.enter_context(nc.sbuf_tensor("c1all", [128, 16], F32))
            c2all = slot_scope.enter_context(nc.sbuf_tensor("c2all", [128, 16], F32))
            with ExitStack() as es:
                xn2T = es.enter_context(nc.sbuf_tensor("xn2T", [128, NMC, 2048], BF16))
                gwsb = es.enter_context(nc.sbuf_tensor("gwsb", [128, NMC, 8], BF16))
                comb = es.enter_context(nc.sbuf_tensor("comb", [128, 16, 8], F32))
                m1b = es.enter_context(nc.sbuf_tensor("m1b", [128, 16, 8], BF16))
                cmb = es.enter_context(nc.sbuf_tensor("cmb", [128, 16, 8], BF16))
                idf = es.enter_context(nc.sbuf_tensor("idf", [128, 128], F32))
                idb = es.enter_context(nc.sbuf_tensor("idb", [128, 128], BF16))
                eoff = es.enter_context(nc.sbuf_tensor("eoff", [128, 8], F32))
                cmT = es.enter_context(nc.sbuf_tensor("cmT", [8, 2048], F32))
                cmT2 = es.enter_context(nc.sbuf_tensor("cmT2", [8, 2048], F32))
                m1T = es.enter_context(nc.sbuf_tensor("m1T", [8, 2048], F32))
                posT = es.enter_context(nc.sbuf_tensor("posT", [128, 16, 8], F32))
                flatm = es.enter_context(nc.sbuf_tensor("flatm", [128, 8], F32))
                ovf = es.enter_context(nc.sbuf_tensor("ovf", [128, 8], F32))
                m2tk = es.enter_context(nc.sbuf_tensor("m2tk", [128, 8], F32))
                sltf = es.enter_context(nc.sbuf_tensor("sltf", [128, 2, 16], F32))
                tp1 = es.enter_context(nc.sbuf_tensor("tp1", [128, 8], F32))
                tp2 = es.enter_context(nc.sbuf_tensor("tp2", [128, 8], F32))
                tl1 = es.enter_context(nc.sbuf_tensor("tl1", [128, 1], F32))
                tl2 = es.enter_context(nc.sbuf_tensor("tl2", [128, 1], F32))
                tc1 = es.enter_context(nc.sbuf_tensor("tc1", [128, 1], F32))
                tc2 = es.enter_context(nc.sbuf_tensor("tc2", [128, 1], F32))
                xn2tl = es.enter_context(nc.sbuf_tensor("xn2tl", [128, 2, M], BF16))
                lgps = es.enter_context(nc.psum_tensor("p4lg", [128, 512], F32))
                trps = es.enter_context(nc.psum_tensor("p4tr", [128, 2, 1024], BF16))
                trpf = es.enter_context(nc.psum_tensor("p4trf", [128, 2, 512], F32))

                @blk.gpsimd
                def _(gpsimd):
                    qG.dinc(gpsimd.dma_start(
                        out=gwsb[:, :, :],
                        in_=gwd.rearrange("(mc p) e -> p mc e", p=128)))
                    qG.dinc(gpsimd.dma_start(out=idf[:, :], in_=idf32d[:, :]))
                    qG.dinc(gpsimd.dma_start(out=idb[:, :], in_=idf32d[:, :]))
                    qG.dinc(gpsimd.dma_start(out=eoff[:, :], in_=eoffd[:, :]))

                cons_ev = qG.n

                @blk.sync
                def _(sync):
                    for mc in range(NMC):
                        qA.dinc(sync.dma_start(
                            out=xn2T[:, mc, :],
                            in_=xn2scr[:, mc * 128:(mc + 1) * 128],
                            transpose=True))

                xn2Tev = qA.n

                lgcpev = []
                for t in range(NOWN):
                    @blk.tensor
                    def _(tensor):
                        tensor.wait_ge(qA.h, xn2Tev)
                        if t == 0:
                            tensor.wait_ge(qG.h, cons_ev)
                        else:
                            tensor.wait_ge(eV.h, lgcpev[t - 1])
                        for mc in range(NMC):
                            ins = tensor.matmul(lgps[:, 0:8],
                                                xn2T[:, mc, t * 128:(t + 1) * 128],
                                                gwsb[:, mc, :],
                                                start=(mc == 0), stop=(mc == NMC - 1))
                        eT.inc(ins)

                    lgev = eT.n

                    @blk.vector
                    def _(vector):
                        vector.wait_ge(eT.h, lgev)
                        eV.inc(vector.tensor_copy(tp1[:, :], lgps[:, 0:8]))
                        lgcpev.append(eV.n)
                        vector.wait_ge(eV.h, eV.n)
                        eV.inc(vector.reduce_max(tl1[:, :], tp1[:, :], axis=AX.X))
                        vector.wait_ge(eV.h, eV.n)
                        eV.inc(vector.tensor_scalar(comb[:, t, :], tp1[:, :],
                                                    tl1[:, 0:1], None, ALU.is_equal))
                        vector.wait_ge(eV.h, eV.n)
                        eV.inc(vector.tensor_copy(m1b[:, t, :], comb[:, t, :]))
                        eV.inc(vector.scalar_tensor_tensor(tp2[:, :], comb[:, t, :],
                                                           -1e30, tp1[:, :],
                                                           ALU.mult, ALU.add))
                        vector.wait_ge(eV.h, eV.n)
                        eV.inc(vector.reduce_max(tl2[:, :], tp2[:, :], axis=AX.X))
                        vector.wait_ge(eV.h, eV.n)
                        eV.inc(vector.tensor_scalar(tp2[:, :], tp2[:, :], tl2[:, 0:1],
                                                    None, ALU.is_equal))
                        vector.wait_ge(eV.h, eV.n)
                        eV.inc(vector.tensor_scalar(tc1[:, :], tl1[:, :], tl2[:, 0:1],
                                                    None, ALU.subtract))

                    vge = eV.n

                    @blk.scalar
                    def _(scalar):
                        scalar.wait_ge(eV.h, vge)
                        eS.inc(scalar.activation(tc1[:, :], tc1[:, :], AF.Sigmoid))

                    sge = eS.n

                    @blk.vector
                    def _(vector):
                        vector.wait_ge(eS.h, sge)
                        eV.inc(vector.tensor_scalar(tc2[:, :], tc1[:, :], -1.0, 1.0,
                                                    ALU.mult, ALU.add))
                        vector.wait_ge(eV.h, eV.n)
                        eV.inc(vector.tensor_copy(c1all[:, t:t + 1], tc1[:, :]))
                        eV.inc(vector.tensor_copy(c2all[:, t:t + 1], tc2[:, :]))
                        eV.inc(vector.tensor_scalar(comb[:, t, :], comb[:, t, :],
                                                    tc1[:, 0:1], None, ALU.mult))
                        vector.wait_ge(eV.h, eV.n)
                        eV.inc(vector.scalar_tensor_tensor(comb[:, t, :], tp2[:, :],
                                                           tc2[:, 0:1], comb[:, t, :],
                                                           ALU.mult, ALU.add))
                        vector.wait_ge(eV.h, eV.n)
                        eV.inc(vector.tensor_scalar(cmb[:, t, :], comb[:, t, :], 0.0,
                                                    None, ALU.is_gt))

                combev = eV.n

                trev = []
                for t in range(NOWN):
                    @blk.tensor
                    def _(tensor):
                        if t == 0:
                            tensor.wait_ge(eV.h, combev)
                        else:
                            tensor.wait_ge(eV.h, trev[t - 1])
                        tensor.transpose(trps[0:8, 0, 0:128], cmb[:, t, :], idb[:, :])
                        ins = tensor.transpose(trps[0:8, 1, 0:128], m1b[:, t, :], idb[:, :])
                        eT.inc(ins)

                    myev = eT.n

                    @blk.vector
                    def _(vector):
                        vector.wait_ge(eT.h, myev)
                        vector.tensor_copy(cmT[0:8, t * 128:(t + 1) * 128],
                                           trps[0:8, 0, 0:128])
                        ins = vector.tensor_copy(m1T[0:8, t * 128:(t + 1) * 128],
                                                 trps[0:8, 1, 0:128])
                        eV.inc(ins)

                    trev.append(eV.n)

                @blk.vector
                def _(vector):
                    vector.wait_ge(eV.h, trev[-1])
                    bufs = [cmT, cmT2]
                    k = 1
                    while k < 2048:
                        a, bnxt = bufs[0], bufs[1]
                        vector.wait_ge(eV.h, eV.n)
                        eV.inc(vector.tensor_copy(bnxt[0:8, 0:k], a[0:8, 0:k]))
                        eV.inc(vector.tensor_add(bnxt[0:8, k:2048], a[0:8, k:2048],
                                                 a[0:8, 0:2048 - k]))
                        bufs = [bnxt, a]
                        k *= 2
                    vector.wait_ge(eV.h, eV.n)
                    eV.inc(vector.tensor_copy(cmT[0:8, :], bufs[0][0:8, :]))

                cumev = eV.n

                sltev = []
                for t in range(NOWN):
                    @blk.tensor
                    def _(tensor):
                        if t == 0:
                            tensor.wait_ge(eV.h, cumev)
                            tensor.wait_ge(qG.h, cons_ev)
                        else:
                            tensor.wait_ge(eV.h, sltev[-1])
                        ins = tensor.transpose(trpf[:, t % 2, 0:8],
                                               cmT[0:8, t * 128:(t + 1) * 128],
                                               idf[0:8, 0:8])
                        eT.inc(ins)

                    pev = eT.n

                    @blk.vector
                    def _(vector):
                        vector.wait_ge(eT.h, pev)
                        eV.inc(vector.tensor_tensor(flatm[:, :], trpf[:, t % 2, 0:8],
                                                    eoff[:, :], ALU.add))
                        eV.inc(vector.tensor_scalar(ovf[:, :], trpf[:, t % 2, 0:8],
                                                    float(CAP) + 0.5, None, ALU.is_ge))
                        vector.wait_ge(eV.h, eV.n)
                        eV.inc(vector.scalar_tensor_tensor(flatm[:, :], ovf[:, :],
                                                           1e9, flatm[:, :],
                                                           ALU.mult, ALU.add))
                        eV.inc(vector.tensor_scalar(flatm[:, :], flatm[:, :], -1.0,
                                                    None, ALU.add))
                        vector.wait_ge(eV.h, eV.n)
                        eV.inc(vector.tensor_tensor(m2tk[:, :], m1b[:, t, :],
                                                    flatm[:, :], ALU.mult))
                        vector.wait_ge(eV.h, eV.n)
                        eV.inc(vector.reduce_sum(sltf[:, 0, t:t + 1], m2tk[:, :],
                                                 axis=AX.X))
                        eV.inc(vector.tensor_sub(m2tk[:, :], cmb[:, t, :], m1b[:, t, :]))
                        vector.wait_ge(eV.h, eV.n)
                        eV.inc(vector.tensor_mul(m2tk[:, :], m2tk[:, :], flatm[:, :]))
                        vector.wait_ge(eV.h, eV.n)
                        ins = vector.reduce_sum(sltf[:, 1, t:t + 1], m2tk[:, :],
                                                axis=AX.X)
                        eV.inc(ins)

                    sltev.append(eV.n)

                @blk.vector
                def _(vector):
                    vector.wait_ge(eV.h, sltev[-1])
                    eV.inc(vector.tensor_copy(slot1i[:, :], sltf[:, 0, :]))
                    eV.inc(vector.tensor_copy(slot2i[:, :], sltf[:, 1, :]))

                slotev = eV.n

                scatev = []
                xn2lev = []
                for t in range(NOWN):
                    @blk.sync
                    def _(sync):
                        if t >= 2:
                            sync.wait_ge(qG.h, scatev[t - 2])
                        xn2lev.append(qS.dinc(sync.dma_start(
                            out=xn2tl[:, t % 2, :],
                            in_=xn2scr[t * 128:(t + 1) * 128, :])))

                    @blk.gpsimd
                    def _(gpsimd):
                        gpsimd.wait_ge(qS.h, xn2lev[t])
                        if t == 0:
                            gpsimd.wait_ge(eV.h, slotev)
                        qG.dinc(gpsimd.indirect_dma_start(
                            out=xescr[:, :],
                            out_offset=bass.IndirectOffsetOnAxis(slot1i[:, t:t + 1], 0),
                            in_=xn2tl[:, t % 2, :], in_offset=None))
                        qG.dinc(gpsimd.indirect_dma_start(
                            out=xescr[:, :],
                            out_offset=bass.IndirectOffsetOnAxis(slot2i[:, t:t + 1], 0),
                            in_=xn2tl[:, t % 2, :], in_offset=None))

                    scatev.append(qG.n)

                gate_done = (eT.n, eV.n, eS.n, qG.n, qA.n, qS.n)

            NCT = CAP // 128
            if p4lvl < 2:
                return
            with ExitStack() as es:
                xeT = es.enter_context(nc.sbuf_tensor("xeT", [128, 2, NMC, CAP], BF16))
                abuf = es.enter_context(nc.sbuf_tensor("abuf", [128, NFT, CAP], BF16))
                wd_sb = es.enter_context(nc.sbuf_tensor("wd_sb", [128, NFT, 2048], BF16))
                wg_sb = es.enter_context(nc.sbuf_tensor("wg_sb", [128, 2, NMC, 128], BF16))
                wu_sb = es.enter_context(nc.sbuf_tensor("wu_sb", [128, 2, NMC, 128], BF16))
                wgf32 = es.enter_context(nc.sbuf_tensor("wgf32", [128, 2, NMC, 128], F32))
                wuf32 = es.enter_context(nc.sbuf_tensor("wuf32", [128, 2, NMC, 128], F32))
                sgbuf = es.enter_context(nc.sbuf_tensor("sgbuf", [128, 2, CAP], F32))
                ybuf = es.enter_context(nc.sbuf_tensor("ybuf", [128, 2, 2048], BF16))
                psA = es.enter_context(nc.psum_tensor("psA", [128, 2048], F32))
                psB = es.enter_context(nc.psum_tensor("psB", [128, 2048], F32))

                xelev = {}
                dn_mm, yevac, ystev, amulev, siluev, gmmev, ummev, wdev_ = \
                    {}, {}, {}, {}, {}, {}, {}, {}
                wgld, wuld, wgc, wuc = {}, {}, {}, {}
                NK = E * NFT
                seqk = [(e2, f2) for e2 in range(E) for f2 in range(NFT)]

                def emit_wg_load(k):
                    if k >= NK:
                        return
                    e2, f2 = seqk[k]

                    @blk.sync
                    def _(sync):
                        if k >= 2:
                            sync.wait_ge(eV.h, wgc[k - 2])
                        else:
                            sync.wait_ge(qG.h, gate_done[3])
                        wgld[k] = qS.dinc(sync.dma_start(
                            out=wgf32[:, k % 2, :, :],
                            in_=wgd[e2, :, f2 * 128:(f2 + 1) * 128].rearrange(
                                "(mc p) f -> p mc f", p=128)))

                def emit_wu_load(k):
                    if k >= NK:
                        return
                    e2, f2 = seqk[k]

                    @blk.scalar
                    def _(scalar):
                        if k >= 2:
                            scalar.wait_ge(eS.h, wuc[k - 2])
                        else:
                            scalar.wait_ge(qG.h, gate_done[3])
                        wuld[k] = qA.dinc(scalar.dma_start(
                            out=wuf32[:, k % 2, :, :],
                            in_=wud[e2, :, f2 * 128:(f2 + 1) * 128].rearrange(
                                "(mc p) f -> p mc f", p=128)))

                def emit_wg_cast(k):
                    if k >= NK or k in wgc:
                        return

                    @blk.vector
                    def _(vector):
                        vector.wait_ge(qS.h, wgld[k])
                        if k >= 2:
                            vector.wait_ge(eT.h, ummev[seqk[k - 2]])
                        else:
                            vector.wait_ge(qG.h, gate_done[3])
                        wgc[k] = eV.inc(vector.tensor_copy(
                            wg_sb[:, k % 2, :, :], wgf32[:, k % 2, :, :]))

                def emit_wu_cast(k):
                    if k >= NK or k in wuc:
                        return

                    @blk.scalar
                    def _(scalar):
                        scalar.wait_ge(qA.h, wuld[k])
                        if k >= 2:
                            scalar.wait_ge(eT.h, ummev[seqk[k - 2]])
                        else:
                            scalar.wait_ge(qG.h, gate_done[3])
                        wuc[k] = eS.inc(scalar.activation(
                            wu_sb[:, k % 2, :, :], wuf32[:, k % 2, :, :], AF.Identity))

                def emit_xeT(e2):
                    @blk.sync
                    def _(sync):
                        if e2 == 0:
                            sync.wait_ge(qG.h, gate_done[3])
                            sync.wait_ge(eT.h, gate_done[0])
                        elif e2 >= 2:
                            sync.wait_ge(eT.h, ummev[(e2 - 2, NFT - 1)])
                        for mc in range(NMC):
                            qA.dinc(sync.dma_start(
                                out=xeT[:, e2 % 2, mc, :],
                                in_=xescr[e2 * CAP:(e2 + 1) * CAP,
                                          mc * 128:(mc + 1) * 128],
                                transpose=True))

                    xelev[e2] = qA.n

                for k0 in range(2):
                    emit_wg_load(k0)
                    emit_wu_load(k0)
                emit_wg_cast(0)
                emit_wu_cast(0)
                emit_xeT(0)
                emit_xeT(1)

                for e in range(E):
                    for ft in range(NFT):
                        k = e * NFT + ft
                        pf = k % 2

                        if ft == 2:
                            @blk.gpsimd
                            def _(gpsimd):
                                if e >= 1:
                                    gpsimd.wait_ge(eT.h, dn_mm[(e - 1, NCT - 1)])
                                for fc in range(NFT):
                                    qG.dinc(gpsimd.dma_start(
                                        out=wd_sb[:, fc, :],
                                        in_=wdd[e, fc * 128:(fc + 1) * 128, :]))

                            wdev_[e] = qG.n

                        emit_wg_load(k + 2)

                        gups = psA if pf == 0 else psB

                        @blk.tensor
                        def _(tensor):
                            tensor.wait_ge(eV.h, wgc[k])
                            tensor.wait_ge(eS.h, wuc[k])
                            if ft == 0:
                                tensor.wait_ge(qA.h, xelev[e])
                            if k >= 2:
                                tensor.wait_ge(eS.h, siluev[seqk[k - 2]])
                                tensor.wait_ge(eV.h, amulev[seqk[k - 2]])
                            if ft == 0:
                                prevy = ystev.get((e - 1, NCT - 1)) if e >= 1 else None
                                if prevy:
                                    tensor.wait_ge(qG.h, prevy)
                            CSL = [(i, min(i + 512, CAP)) for i in range(0, CAP, 512)]
                            for mc in range(NMC):
                                for lo, hi in CSL:
                                    ins = tensor.matmul(
                                        gups[:, lo:hi],
                                        wg_sb[:, pf, mc, :],
                                        xeT[:, e % 2, mc, lo:hi],
                                        start=(mc == 0), stop=(mc == NMC - 1))
                            eT.inc(ins)
                            gmmev[(e, ft)] = eT.n
                            for mc in range(NMC):
                                for lo, hi in CSL:
                                    ins = tensor.matmul(
                                        gups[:, 1024 + lo:1024 + hi],
                                        wu_sb[:, pf, mc, :],
                                        xeT[:, e % 2, mc, lo:hi],
                                        start=(mc == 0), stop=(mc == NMC - 1))
                            eT.inc(ins)
                            ummev[(e, ft)] = eT.n

                        emit_wu_cast(k + 1)
                        emit_wu_load(k + 2)

                        @blk.scalar
                        def _(scalar):
                            scalar.wait_ge(eT.h, gmmev[(e, ft)])
                            if k >= 2:
                                scalar.wait_ge(eV.h, amulev[seqk[k - 2]])
                            eS.inc(scalar.activation(sgbuf[:, pf, :], gups[:, 0:CAP],
                                                     AF.Silu))

                        siluev[(e, ft)] = eS.n

                        emit_wg_cast(k + 1)

                        @blk.vector
                        def _(vector):
                            vector.wait_ge(eS.h, siluev[(e, ft)])
                            vector.wait_ge(eT.h, ummev[(e, ft)])
                            if e >= 1 and ft == 0:
                                vector.wait_ge(eT.h, dn_mm[(e - 1, NCT - 1)])
                            ins = vector.tensor_tensor(abuf[:, ft, :],
                                                       gups[:, 1024:1024 + CAP],
                                                       sgbuf[:, pf, :], ALU.mult)
                            eV.inc(ins)

                        amulev[(e, ft)] = eV.n

                    if e + 2 < E:
                        emit_xeT(e + 2)
                    pfl = (e * NFT + NFT - 1) % 2
                    for u in range(NCT):
                        yph = psA if (pfl + u) % 2 == 0 else psB

                        @blk.tensor
                        def _(tensor):
                            if u == 0:
                                tensor.wait_ge(qG.h, wdev_[e])
                                tensor.wait_ge(eV.h, amulev[(e, NFT - 1)])
                                tensor.wait_ge(eS.h, siluev[(e, NFT - 1)])
                            if u == 1:
                                tensor.wait_ge(eV.h, amulev[(e, NFT - 2)])
                                tensor.wait_ge(eS.h, siluev[(e, NFT - 2)])
                            if u == 1 and e >= 1:
                                prevy = ystev.get((e - 1, NCT - 1))
                                if prevy:
                                    tensor.wait_ge(qG.h, prevy)
                            prevy = ystev.get((e, u - 2)) if u >= 2 else None
                            if prevy:
                                tensor.wait_ge(qG.h, prevy)
                            for fc in range(NFT):
                                for q4 in range(4):
                                    ins = tensor.matmul(
                                        yph[:, q4 * 512:(q4 + 1) * 512],
                                        abuf[:, fc, u * 128:(u + 1) * 128],
                                        wd_sb[:, fc, q4 * 512:(q4 + 1) * 512],
                                        start=(fc == 0), stop=(fc == NFT - 1))
                            eT.inc(ins)

                        dn_mm[(e, u)] = eT.n

                        @blk.scalar
                        def _(scalar):
                            scalar.wait_ge(eT.h, dn_mm[(e, u)])
                            prevst = ystev.get((e, u - 2)) if u >= 2 else \
                                (ystev.get((e - 1, NCT - 2 + u)) if e >= 1 else None)
                            if prevst:
                                scalar.wait_ge(qG.h, prevst)
                            eS.inc(scalar.activation(ybuf[:, u % 2, :], yph[:, :],
                                                     AF.Identity))

                        yevac[(e, u)] = eS.n

                        @blk.gpsimd
                        def _(gpsimd):
                            gpsimd.wait_ge(eS.h, yevac[(e, u)])
                            ystev[(e, u)] = qG.dinc(gpsimd.dma_start(
                                out=yescr[e * CAP + u * 128:e * CAP + (u + 1) * 128, :],
                                in_=ybuf[:, u % 2, :]))

                ye_done = (qG.n, eS.n)

            if p4lvl < 3:
                return
            with ExitStack() as es:
                DC = 4
                y1sb = es.enter_context(nc.sbuf_tensor("y1sb", [128, DC, M], BF16))
                y2sb = es.enter_context(nc.sbuf_tensor("y2sb", [128, DC, M], BF16))
                xmtl = es.enter_context(nc.sbuf_tensor("xmtl", [128, DC, M], F32))
                otl = es.enter_context(nc.sbuf_tensor("otl", [128, DC, M], F32))

                gthev = {}
                xmlev = {}
                cmbv = {}
                outev = {}

                for t in range(NOWN):
                    pt = t % DC

                    @blk.gpsimd
                    def _(gpsimd):
                        if t == 0:
                            gpsimd.wait_ge(qG.h, ye_done[0])
                        if t >= DC:
                            gpsimd.wait_ge(eV.h, cmbv[t - DC])
                        qG.dinc(gpsimd.indirect_dma_start(
                            out=y1sb[:, pt, :], out_offset=None, in_=yescr[:, :],
                            in_offset=bass.IndirectOffsetOnAxis(slot1i[:, t:t + 1], 0)))
                        qG.dinc(gpsimd.indirect_dma_start(
                            out=y2sb[:, pt, :], out_offset=None, in_=yescr[:, :],
                            in_offset=bass.IndirectOffsetOnAxis(slot2i[:, t:t + 1], 0)))

                    gthev[t] = qG.n

                    @blk.scalar
                    def _(scalar):
                        if t == 0:
                            scalar.wait_ge(eS.h, ye_done[1])
                        if t >= DC:
                            scalar.wait_ge(eV.h, cmbv[t - DC])
                        xmlev[t] = qA.dinc(scalar.dma_start(
                            out=xmtl[:, pt, :],
                            in_=xmidscr[t * 128:(t + 1) * 128, :]))

                    @blk.vector
                    def _(vector):
                        vector.wait_ge(qG.h, gthev[t])
                        vector.wait_ge(qA.h, xmlev[t])
                        if t >= DC:
                            vector.wait_ge(qS.h, outev[t - DC])
                        eV.inc(vector.scalar_tensor_tensor(
                            otl[:, pt, :], y1sb[:, pt, :], c1all[:, t:t + 1],
                            xmtl[:, pt, :], ALU.mult, ALU.add))
                        ins = vector.scalar_tensor_tensor(
                            otl[:, pt, :], y2sb[:, pt, :], c2all[:, t:t + 1],
                            otl[:, pt, :], ALU.mult, ALU.add)
                        eV.inc(ins)

                    cmbv[t] = eV.n

                    @blk.sync
                    def _(sync):
                        sync.wait_ge(eV.h, cmbv[t])
                        outev[t] = qS.dinc(sync.dma_start(
                            out=outd[t * 128:(t + 1) * 128, :], in_=otl[:, pt, :]))

        if phases >= 4:
            phase4()

        @blk.sync
        def _(sync):
            sync.wait_ge(qS.h, qS.n)
            sync.wait_ge(qA.h, qA.n)
            sync.wait_ge(qG.h, qG.n)

    return nc


def _prep_inputs(core, inputs):
    x = np.asarray(inputs["L0_attn_x"])
    b, c = core // 4, core % 4
    taus = [4 * s + c for s in range(16)]
    rows = np.concatenate([np.arange(t * 128, (t + 1) * 128) for t in taus])

    w_aq = np.asarray(inputs["L0_attn_w_aq"])
    w_ak = np.asarray(inputs["L0_attn_w_ak"])
    w_av = np.asarray(inputs["L0_attn_w_av"])
    w_ao = np.asarray(inputs["L0_attn_w_ao"])

    wq = np.ascontiguousarray(np.transpose(w_aq, (2, 0, 1, 3)).reshape(M, G * H * DH))
    wkv = np.ascontiguousarray(np.concatenate(
        [np.transpose(w_ak, (1, 0, 2)).reshape(M, G * DH),
         np.transpose(w_av, (1, 0, 2)).reshape(M, G * DH)], axis=1))
    wao = np.ascontiguousarray(w_ao.reshape(G * H, DH, M))

    freqs = np.float32(10000.0) ** (-np.arange(64, dtype=np.float32) / np.float32(64.0))
    posk = np.arange(T, dtype=np.float32)
    angk = posk[:, None] * freqs[None, :]
    ck = np.ascontiguousarray(np.tile(np.cos(angk).astype(np.float32), (1, G)))
    sk = np.ascontiguousarray(np.tile(np.sin(angk).astype(np.float32), (1, G)))

    posq = rows.astype(np.float32)
    angq = posq[:, None] * freqs[None, :]
    qk2 = np.float32(np.float32(DH ** -0.25) * np.float32(DH ** -0.25))
    cq = np.ascontiguousarray(np.tile((np.cos(angq) * qk2).astype(np.float32), (1, G * H)))
    sq = np.ascontiguousarray(np.tile((np.sin(angq) * qk2).astype(np.float32), (1, G * H)))

    mk = np.zeros((NSP, 8, 128, 256), dtype=np.float32)
    jj = np.arange(128)
    for sp in range(NSP):
        for mcl in range(8):
            mc = 8 * sp + mcl
            jglob = mc * 128 + jj
            for tt in range(2):
                tau = 4 * (2 * sp + tt) + c
                iglob = tau * 128 + np.arange(128)
                mk[sp, mcl, :, tt * 128:(tt + 1) * 128] = \
                    (jglob[:, None] <= iglob[None, :])
    masks = mk.astype(ml_dtypes.bfloat16)

    return {
        "xb": np.ascontiguousarray(x[b]),
        "xown": np.ascontiguousarray(x[b][rows]),
        "wq": wq, "wkv": wkv, "wao": wao,
        "gw": np.ascontiguousarray(np.asarray(inputs["L0_ffn_gate_w"])),
        "wgt": np.ascontiguousarray(np.asarray(inputs["L0_ffn_w_gate"])),
        "wup": np.ascontiguousarray(np.asarray(inputs["L0_ffn_w_up"])),
        "wdn": np.ascontiguousarray(np.asarray(inputs["L0_ffn_w_down"])),
        "ck": ck, "sk": sk, "cq": cq, "sq": sq,
        "masks": masks,
        "ones_in": np.ones((128, 128), dtype=ml_dtypes.bfloat16),
        "idf32": np.eye(128, dtype=np.float32),
        "eoffr": np.tile((np.arange(8, dtype=np.float32) * np.float32(CAP))[None, :],
                         (128, 1)),
    }, rows, b


def kernel(**inputs):
    nc = build_program()
    in_maps = []
    metas = []
    for core in range(8):
        im, rows, b = _prep_inputs(core, inputs)
        in_maps.append(im)
        metas.append((rows, b))
    res = run_bass_kernel_spmd(nc, in_maps, core_ids=list(range(8)))
    out = np.empty((B, T, M), dtype=np.float32)
    for core in range(8):
        rows, b = metas[core]
        out[b][rows] = res.results[core]["out"]
    return out
